# version 18
# baseline (speedup 1.0000x reference)
import numpy as np
from contextlib import ExitStack

import concourse.bass as bass
import concourse.mybir as mybir
from concourse.bass_utils import run_bass_kernel_spmd

F32 = mybir.dt.float32
BF16 = mybir.dt.bfloat16
AF = mybir.ActivationFunctionType
ALU = mybir.AluOpType
AX = mybir.AxisListType

D = 1024
T = 2048
NSEQ = 2
DC = 8
NH, NKV, HD = 16, 4, 64
DFF = 2752
DFFP = 2816
FC = DFFP // 128
NST = 16
KCONV = 4
DXP = 1152
DH = 4096
OC_H = DH // 128
NEG = -1.0e30

TCA = 512
TCF = 512
TCM = 256

_PHASES = 7
_DUMP = False
_MSTAGE = 9
_M2 = 9
_ONLY_M0 = False
_ONLY_P7 = False
_STRICT_SAME_ENGINE = False
_ARDBG = False
_P7CH = None

ENG = ("pe", "act", "dve", "pool", "sp")


class Buf:
    __slots__ = ("name", "wr", "rd", "small", "psum")

    def __init__(self, name):
        self.name = name
        self.wr = None
        self.rd = []
        self.small = False
        self.psum = name.startswith("ps")


class Op:
    __slots__ = ("eng", "fn", "waits", "inc")

    def __init__(self, eng, fn, waits, inc):
        self.eng = eng
        self.fn = fn
        self.waits = waits
        self.inc = inc


class KB:

    def __init__(self):
        self.ops = {e: [] for e in ENG}
        self.sem_names = ["pe", "act", "dve", "pool"]
        self.dma_sems = set()
        self.group_sems = set()
        self.cnt = {s: 0 for s in self.sem_names}
        self.waited = {e: {} for e in ENG}
        self.pending = {e: [] for e in ENG}
        self.out_tokens = []

    def _sem(self, name):
        if name not in self.cnt:
            self.sem_names.append(name)
            self.cnt[name] = 0
        return name

    def _emit(self, eng, fn, deps, inc_sem, inc_amt, fences=()):
        waits = []
        allw = [(s, v, False) for (s, v) in list(self.pending[eng]) + list(deps)]
        allw += [(s, v, True) for (s, v) in fences]
        self.pending[eng] = []
        if inc_sem in self.group_sems:
            allw = [(s, v, f) for (s, v, f) in allw if s != inc_sem]
        for (s, v, force) in allw:
            if v is None or v <= 0:
                continue
            if s == eng and not force and not _STRICT_SAME_ENGINE:
                continue
            if s in self.dma_sems:
                v = self.cnt[s]
            if self.waited[eng].get(s, 0) >= v:
                continue
            self.waited[eng][s] = v
            waits.append((s, v))
        inc = None
        if inc_sem is not None:
            self.cnt[inc_sem] += inc_amt
            inc = (inc_sem, inc_amt)
        self.ops[eng].append(Op(eng, fn, waits, inc))
        return (inc_sem, self.cnt[inc_sem]) if inc_sem else None

    @staticmethod
    def _deps_of(reads, writes):
        deps = []
        for b in reads:
            if b.wr:
                deps.append(b.wr)
        for b in writes:
            if b.wr:
                deps.append(b.wr)
            deps.extend(b.rd)
        return deps

    def op(self, eng, fn, reads=(), writes=(), tag="", small=False):
        deps = self._deps_of(reads, writes)
        for b in reads:
            if b.psum:
                deps = list(deps) + list(b.rd)
        fences = [b.wr for b in reads
                  if b.wr is not None and b.small and b.wr[0] == eng]
        sem = eng if eng != "sp" else "spd"
        tok = self._emit(eng, fn, deps, sem, 1, fences=fences)
        for b in reads:
            b.rd.append(tok)
        for b in writes:
            b.wr = tok
            b.rd = []
            b.small = small
        return tok

    def dma(self, eng, fn, reads=(), writes=(), out_is_output=False,
            stream=None):
        deps = self._deps_of(reads, writes)
        for b in reads:
            if b.psum:
                deps = list(deps) + list(b.rd)
        key = stream if stream is not None else (writes[0].name if writes else "out")
        sem = self._sem(f"d_{eng[0]}_{key}")
        self.dma_sems.add(sem)
        if stream is not None:
            self.group_sems.add(sem)
        tok = self._emit(eng, fn, deps, sem, 16)
        for b in reads:
            b.rd.append(tok)
        for b in writes:
            b.wr = tok
            b.rd = []
        if out_is_output:
            self.out_tokens.append(tok)
        return tok

    def mm_group(self, mm_list, reads=(), psb=None, tag=""):
        n = len(mm_list)
        assert n >= 1
        deps = self._deps_of(reads, (psb,) if psb else ())
        tok = None
        for i, (o, l, r) in enumerate(mm_list):
            st, sp_ = (i == 0), (i == n - 1)
            fn = (lambda e, o=o, l=l, r=r, st=st, sp_=sp_:
                  e.matmul(o, l, r, start=st, stop=sp_))
            tok = self._emit("pe", fn, deps if i == 0 else (),
                             "pe" if sp_ else None, 1)
        for b in reads:
            b.rd.append(tok)
        if psb is not None:
            psb.wr = tok
            psb.rd = []
        return tok

    def mm1(self, out_ap, l_ap, r_ap, start, stop, reads=(), psb=None):
        deps = self._deps_of(reads, (psb,) if psb is not None else ())
        fn = (lambda e, o=out_ap, l=l_ap, r=r_ap, st=start, sp_=stop:
              e.matmul(o, l, r, start=st, stop=sp_))
        tok = self._emit("pe", fn, deps, "pe", 1)
        for b in reads:
            b.rd.append(tok)
        if psb is not None:
            psb.wr = tok
            psb.rd = []
        return tok

    def transpose(self, out_ap, in_ap, ident_ap, reads=(), psb=None):
        deps = self._deps_of(reads, (psb,) if psb is not None else ())
        fn = (lambda e, o=out_ap, i=in_ap, d=ident_ap: e.transpose(o, i, d))
        tok = self._emit("pe", fn, deps, "pe", 1)
        for b in reads:
            b.rd.append(tok)
        if psb is not None:
            psb.wr = tok
            psb.rd = []
        return tok

    def barrier(self):
        toks = [(s, self.cnt[s]) for s in self.sem_names if self.cnt[s] > 0]
        for e in ENG:
            self.pending[e].extend(toks)

    def finalize(self):
        for e in ENG:
            for op_ in self.ops[e]:
                op_.waits = [(sm, self.cnt[sm] if sm in self.group_sems else v)
                             for (sm, v) in op_.waits]
        waits = []
        for s in sorted(self.dma_sems):
            v = self.cnt[s]
            if v > 0 and self.waited["sp"].get(s, 0) < v:
                self.waited["sp"][s] = v
                waits.append((s, v))
        self.ops["sp"].append(Op("sp", None, waits, None))



def _prep_inputs(inp):
    import ml_dtypes
    W = {}
    W["wq"] = np.ascontiguousarray(inp["L0_attn_Wq"][0])
    W["wk"] = np.ascontiguousarray(inp["L0_attn_Wk"][0])
    W["wv"] = np.ascontiguousarray(inp["L0_attn_Wv"][0])
    W["wo"] = np.ascontiguousarray(inp["L0_attn_Wo"][0])
    W["wg"] = np.ascontiguousarray(
        np.pad(inp["L0_attn_Wg"][0], ((0, 0), (0, DFFP - DFF))))
    W["wu"] = np.ascontiguousarray(
        np.pad(inp["L0_attn_Wu"][0], ((0, 0), (0, DFFP - DFF))))
    W["wd"] = np.ascontiguousarray(
        np.pad(inp["L0_attn_Wd"][0], ((0, DFFP - DFF), (0, 0))))
    for l in range(3):
        W[f"win{l}"] = np.ascontiguousarray(inp["L0_attn_Win"][l])
        W[f"wx{l}"] = np.ascontiguousarray(
            np.pad(inp["L0_attn_Wx"][l], ((0, 0), (0, DXP - (D + 2 * NST)))))
        W[f"wout{l}"] = np.ascontiguousarray(inp["L0_attn_Wout"][l])
        W[f"cw{l}"] = np.ascontiguousarray(
            inp["L0_attn_conv_w"][l].reshape(DC, 128, KCONV).transpose(1, 0, 2))
        W[f"cb{l}"] = np.ascontiguousarray(
            inp["L0_attn_conv_b"][l].reshape(DC, 128).T)
        W[f"gm{l}"] = np.ascontiguousarray(
            inp["L0_attn_m_norm_w"][l].reshape(DC, 128).T)
        W[f"dp{l}"] = np.ascontiguousarray(
            inp["L0_attn_Dp"][l].reshape(DC, 128).T)
        W[f"alog{l}"] = np.ascontiguousarray(
            inp["L0_attn_A_log"][l].reshape(DC, 128, NST).transpose(1, 0, 2))
    W["gattn"] = np.ascontiguousarray(
        inp["L0_attn_attn_norm_w"][0].reshape(DC, 128).T)
    W["gffn"] = np.ascontiguousarray(
        inp["L0_attn_ffn_norm_w"][0].reshape(DC, 128).T)
    W["w1t"] = np.ascontiguousarray(inp["L0_ffn_w1_w"].T)
    W["w2t"] = np.ascontiguousarray(inp["L0_ffn_w2_w"].T)
    W["h1t"] = np.ascontiguousarray(inp["L0_ffn_h1_w"].T)
    W["h2t"] = np.ascontiguousarray(inp["L0_ffn_h2_w"].T)
    W["w3t"] = np.ascontiguousarray(inp["L0_ffn_w3_w"].T)
    W["s1"] = np.ascontiguousarray(inp["L0_ffn_w1_s"].reshape(OC_H, 128).T)
    W["s2"] = np.ascontiguousarray(inp["L0_ffn_w2_s"].reshape(OC_H, 128).T)
    W["sh1"] = np.ascontiguousarray(inp["L0_ffn_h1_s"].reshape(OC_H, 128).T)
    W["sh2"] = np.ascontiguousarray(inp["L0_ffn_h2_s"].reshape(OC_H, 128).T)
    W["s3"] = np.ascontiguousarray(inp["L0_ffn_w3_s"].reshape(DC, 128).T)
    tri = np.tril(np.ones((128, 128), dtype=bool))
    W["maskd"] = np.where(tri, 0.0, NEG).astype(np.float32)
    W["maskdt"] = np.ascontiguousarray(W["maskd"].T)
    import ml_dtypes
    W["ident"] = np.eye(128, dtype=ml_dtypes.bfloat16)
    return W



def build_program():
    nc = bass.Bass("TRN2", target_bir_lowering=False)
    kb = KB()

    xin_t = nc.dram_tensor("xin", [NSEQ, D, T], F32, kind="ExternalInput")
    y_t = nc.dram_tensor("y", [NSEQ, D, T], F32, kind="ExternalOutput")
    dr = {}
    for nm, shp, dt in [
        ("wq", [D, D], F32), ("wk", [D, 256], F32), ("wv", [D, 256], F32),
        ("wo", [D, D], F32),
        ("wg", [D, DFFP], F32), ("wu", [D, DFFP], F32), ("wd", [DFFP, D], F32),
        ("gattn", [128, DC], F32), ("gffn", [128, DC], F32),
        ("w1t", [D, DH], F32), ("w2t", [D, DH], F32),
        ("h1t", [DH, DH], F32), ("h2t", [DH, DH], F32), ("w3t", [DH, D], F32),
        ("s1", [128, OC_H], F32), ("s2", [128, OC_H], F32),
        ("sh1", [128, OC_H], F32), ("sh2", [128, OC_H], F32),
        ("s3", [128, DC], F32),
        ("maskd", [128, 128], F32), ("maskdt", [128, 128], F32),
        ("ident", [128, 128], BF16),
    ]:
        dr[nm] = nc.dram_tensor(nm, shp, dt, kind="ExternalInput")
    for l in range(3):
        dr[f"win{l}"] = nc.dram_tensor(f"win{l}", [D, 2 * D], F32, kind="ExternalInput")
        dr[f"wx{l}"] = nc.dram_tensor(f"wx{l}", [D, DXP], F32, kind="ExternalInput")
        dr[f"wout{l}"] = nc.dram_tensor(f"wout{l}", [D, D], F32, kind="ExternalInput")
        dr[f"cw{l}"] = nc.dram_tensor(f"cw{l}", [128, DC, KCONV], F32, kind="ExternalInput")
        dr[f"cb{l}"] = nc.dram_tensor(f"cb{l}", [128, DC], F32, kind="ExternalInput")
        dr[f"gm{l}"] = nc.dram_tensor(f"gm{l}", [128, DC], F32, kind="ExternalInput")
        dr[f"dp{l}"] = nc.dram_tensor(f"dp{l}", [128, DC], F32, kind="ExternalInput")
        dr[f"alog{l}"] = nc.dram_tensor(f"alog{l}", [128, DC, NST], F32, kind="ExternalInput")
    xs_a = nc.dram_tensor("xs_a", [NSEQ, D, T], F32)
    xs_b = nc.dram_tensor("xs_b", [NSEQ, D, T], F32)
    w1b = nc.dram_tensor("w1b", [OC_H, 128, DC * 128], BF16)
    w2b = nc.dram_tensor("w2b", [OC_H, 128, DC * 128], BF16)
    h1b = nc.dram_tensor("h1b", [OC_H, 128, OC_H * 128], BF16)
    h2b = nc.dram_tensor("h2b", [OC_H, 128, OC_H * 128], BF16)
    w3b = nc.dram_tensor("w3b", [DC, 128, OC_H * 128], BF16)
    wub = nc.dram_tensor("wub", [FC, 128, DC * 128], BF16)
    wxb = [nc.dram_tensor(f"wxb{l}", [DXP // 128, 128, DC * 128], BF16) for l in range(3)]
    winb = [nc.dram_tensor(f"winb{l}", [16, 128, DC * 128], BF16) for l in range(3)]
    woutb = [nc.dram_tensor(f"woutb{l}", [DC, 128, DC * 128], BF16) for l in range(3)]

    bufs = {}

    def B(name):
        if name not in bufs:
            bufs[name] = Buf(name)
        return bufs[name]

    ctx = ExitStack()
    sbt = {}

    def sb(name, shape, dt):
        assert "__frozen__" not in sbt, f"permanent alloc {name} after arena start"
        h = ctx.enter_context(nc.sbuf_tensor(name, shape, dt))
        sbt[name] = h
        return h

    ident = sb("sident", [128, 128], BF16)
    maskd = sb("smaskd", [128, 128], F32)
    maskdt = sb("smaskdt", [128, 128], F32)
    ones_r = sb("ones_r", [1, 128], BF16)
    ones_r32 = sb("ones_r32", [1, 128], F32)
    ones_c = sb("ones_c", [128, 1], BF16)
    gattn = sb("sgattn", [128, DC], F32)
    gffn = sb("sgffn", [128, DC], F32)
    gm = [sb(f"sgm{l}", [128, DC], F32) for l in range(3)]
    dp = [sb(f"sdp{l}", [128, DC], F32) for l in range(3)]
    cw = [sb(f"scw{l}", [128, DC, KCONV], F32) for l in range(3)]
    cb = [sb(f"scb{l}", [128, DC], F32) for l in range(3)]
    amat = [sb(f"samat{l}", [128, DC, NST], F32) for l in range(3)]
    s1v = sb("ss1", [128, OC_H], F32)
    s2v = sb("ss2", [128, OC_H], F32)
    sh1v = sb("ssh1", [128, OC_H], F32)
    sh2v = sb("ssh2", [128, OC_H], F32)
    s3v = sb("ss3", [128, DC], F32)
    carry = sb("carry", [128, DC, NST], F32)
    halo = sb("halo", [128, DC, KCONV - 1], F32)
    rowst = sb("rowst", [1, 3 * TCF], F32)
    
    cst = sb("cst", [128, 8 * NST], F32)
    cbfa = sb("cbf_a", [128, 16], BF16)
    cbfb = sb("cbf_b", [128, 16], BF16)
    ctr = sb("ctr", [16, 256], BF16)

    psS = ctx.enter_context(nc.psum_tensor("psS", [128, 2048], F32))
    ps0 = ctx.enter_context(nc.psum_tensor("ps0", [128, 512], F32))
    ps1 = ctx.enter_context(nc.psum_tensor("ps1", [128, 512], F32))
    ps2 = ctx.enter_context(nc.psum_tensor("ps2", [128, 512], F32))
    psT = ctx.enter_context(nc.psum_tensor("psT", [128, 1024], BF16))
    PSB = [ps0, ps1, ps2]

    PART_BYTES = nc.SBUF_PARTITION_SIZE_BYTES

    class Arena:
        def __init__(self):
            self.base = None
            self.cur = 0
            self.peak = 0
            self.n = 0

        def start(self):
            if self.base is None:
                self.base = (nc._sbuf_addr_for_side("left") + 31) // 32 * 32
                sbt["__frozen__"] = True

        def alloc(self, name, shape, dt):
            self.start()
            per = int(np.prod(shape[1:])) * mybir.dt.size(dt)
            per = (per + 31) // 32 * 32
            off = self.base + self.cur
            self.cur += per
            self.peak = max(self.peak, self.cur)
            assert off + per <= PART_BYTES, (
                f"sbuf arena overflow: {name} at {off}+{per} > {PART_BYTES}")
            self.n += 1
            h = nc.alloc_sbuf_tensor_at(f"ar_{name}_{self.n}", shape, dt, offset=off)
            return h

        def reset(self):
            if _ARDBG and self.base is not None:
                print(f"[arena] base={self.base} peak={self.peak} "
                      f"free={PART_BYTES - self.base - self.peak}")
            self.cur = 0
            self.peak = 0

    AR = Arena()

    sem_handles = {}

    def t_ap(h):
        return h[:]

    def xin_chunk_ap(s, t0, tc):
        return xin_t.ap()[s].rearrange("(dc p) t -> p dc t", p=128)[:, :, t0:t0 + tc]

    def xs_chunk_ap(xs, s, t0, tc):
        return xs.ap()[s].rearrange("(dc p) t -> p dc t", p=128)[:, :, t0:t0 + tc]

    def w_nat_ap(drt, kcn):
        return drt.ap().rearrange("(kc p) o -> p kc o", p=128)

    for nm, dst in [("ident", ident), ("maskd", maskd), ("maskdt", maskdt),
                    ("gattn", gattn), ("gffn", gffn),
                    ("s1", s1v), ("s2", s2v), ("sh1", sh1v), ("sh2", sh2v),
                    ("s3", s3v)]:
        kb.dma("sp", lambda e, d=dst, s=dr[nm]: e.dma_start(out=d[:], in_=s.ap()),
               writes=(B("c_" + nm),), stream="const")
    for l in range(3):
        for nm, dst in [(f"gm{l}", gm[l]), (f"dp{l}", dp[l]), (f"cw{l}", cw[l]),
                        (f"cb{l}", cb[l])]:
            kb.dma("sp", lambda e, d=dst, s=dr[nm]: e.dma_start(out=d[:], in_=s.ap()),
                   writes=(B("c_" + nm),), stream="const")
        kb.dma("sp", lambda e, d=amat[l], s=dr[f"alog{l}"]:
               e.dma_start(out=d[:], in_=s.ap()), writes=(B(f"amat{l}"),),
               stream="const")
        kb.op("act", lambda e, d=amat[l]: e.activation(out=d[:], in_=d[:], func=AF.Exp),
              reads=(), writes=(B(f"amat{l}"),), small=True)
        kb.op("dve", lambda e, d=amat[l]: e.tensor_scalar_mul(out=d[:], in0=d[:], scalar1=-1.0),
              writes=(B(f"amat{l}"),), small=True)
    kb.op("dve", lambda e: e.memset(ones_r[:], 1.0), writes=(B("ones"),), small=True)
    kb.op("dve", lambda e: e.memset(ones_r32[:], 1.0), writes=(B("ones"),), small=True)
    kb.op("dve", lambda e: e.memset(ones_c[:], 1.0), writes=(B("ones"),), small=True)

    _cv_cache = {}

    def cv(val):
        val = float(val)
        assert val in _cv_cache, f"cv({val}) not pre-registered"
        return _cv_cache[val][:, 0:1]

    def cvB():
        return B("consts")

    for _v in (1.0 / D, 1e-6, 1e-5, -0.5, float(HD) ** -0.5, 1.0, -1.0,
               1.0 / DH, 0.5):
        _v = float(_v)
        if _v not in _cv_cache:
            h = sb(f"cv{len(_cv_cache)}", [128, 1], F32)
            kb.op("dve", lambda e, h=h, v=_v: e.memset(h[:], v), writes=(B("consts"),), small=True)
            _cv_cache[_v] = h


    def rms_norm_chunk(xbuf_h, xB, g_h, gB, xn_h, xnB, sq_h, tc, eps=1e-6,
                       sqB=None):
        if sqB is None:
            sqB = B("sq")
        kb.op("act", lambda e: e.activation(out=sq_h[:], in_=xbuf_h[:], func=AF.Square),
              reads=(xB,), writes=(sqB,))
        mm = [(ps0[0:1, 0:tc], ones_c[:], sq_h[:, kc, :]) for kc in range(DC)]
        kb.mm_group(mm, reads=(sqB, B("ones")), psb=B("ps0"))
        kb.op("act", lambda e, sc=cv(1.0 / D)[0:1, :], bi=cv(eps)[0:1, :]:
              e.activation(out=rowst[0:1, 0:tc], in_=ps0[0:1, 0:tc],
                           func=AF.Ln, scale=sc, bias=bi),
              reads=(B("ps0"), cvB()), writes=(B("rowst"),), small=True)
        kb.op("act", lambda e, sc=cv(-0.5)[0:1, :], tc=tc: e.activation(
            out=rowst[0:1, tc:2 * tc], in_=rowst[0:1, 0:tc], func=AF.Exp, scale=sc),
              reads=(B("rowst"), cvB()), writes=(B("rowst"),), small=True)
        kb.mm_group([(ps1[:, 0:tc], ones_r32[:], rowst[0:1, tc:2 * tc])],
                    reads=(B("rowst"), B("ones")), psb=B("ps1"))
        assert tc <= 2 * TCF
        for dc in range(DC):
            kb.op("dve", lambda e, dc=dc: e.scalar_tensor_tensor(
                out=xn_h[:, dc, :], in0=xbuf_h[:, dc, :], scalar=g_h[:, dc:dc + 1],
                in1=ps1[:, 0:tc], op0=ALU.mult, op1=ALU.mult),
                reads=(xB, B("ps1"), gB), writes=(xnB,))

    def proj_mm(xn_h, xnB, w_h, wB, oc, tc, psb_t, psb_B, kcs=DC):
        mm = [(psb_t[:, 0:tc], w_h[:, kc, oc * 128:(oc + 1) * 128], xn_h[:, kc, 0:tc])
              for kc in range(kcs)]
        kb.mm_group(mm, reads=(xnB, wB), psb=psb_B)

    def precast_outer():
        def pc(src_dr, dst_dr, noc, nkc):
            srcv = src_dr.ap().rearrange("(kc p) (oc o) -> oc p kc o", p=128, o=128)
            nm = dst_dr.name
            for oc in range(noc):
                if nkc <= 16:
                    kb.dma("pool", lambda e, oc=oc, dst_dr=dst_dr, srcv=srcv:
                           e.dma_start(
                        out=dst_dr.ap()[oc].rearrange("p (kc o) -> p kc o", o=128),
                        in_=srcv[oc]), writes=(B(nm),), stream="pre7")
                else:
                    for hh in range(2):
                        k0, k1 = hh * nkc // 2, (hh + 1) * nkc // 2
                        kb.dma("pool", lambda e, oc=oc, k0=k0, k1=k1,
                               dst_dr=dst_dr, srcv=srcv: e.dma_start(
                            out=dst_dr.ap()[oc].rearrange(
                                "p (kc o) -> p kc o", o=128)[:, k0:k1, :],
                            in_=srcv[oc][:, k0:k1, :]), writes=(B(nm),),
                            stream="pre7")
        pc(dr["w1t"], w1b, OC_H, DC)
        pc(dr["w2t"], w2b, OC_H, DC)
        pc(dr["h1t"], h1b, OC_H, OC_H)
        pc(dr["h2t"], h2b, OC_H, OC_H)
        pc(dr["w3t"], w3b, DC, OC_H)

    def phase1():
        AR.reset()
        wq_h = AR.alloc("wqb", [128, DC, D], BF16)
        wk_h = AR.alloc("wkb", [128, DC, 256], BF16)
        wv_h = AR.alloc("wvb", [128, DC, 256], BF16)
        wo_h = AR.alloc("wob", [128, DC, D], BF16)
        Q = AR.alloc("Q", [128, DC, T], BF16)
        Kaug = [AR.alloc(f"Kaug{g}", [66, T], BF16) for g in range(NKV)]
        VT = AR.alloc("VT", [128, 16, 4 * 65], BF16)
        Qaug = [AR.alloc(f"Qaug{i}", [66, T], BF16) for i in range(2)]
        Ofm = AR.alloc("Ofm", [128, DC, T], BF16)
        xch = AR.alloc("xch", [128, DC, TCA], F32)
        xn = AR.alloc("xn", [128, DC, TCA], BF16)
        sq = AR.alloc("sqa", [128, DC, TCA], BF16)
        vtmp = AR.alloc("vtmp", [128, 2, TCA], BF16)
        pt = [AR.alloc(f"pt{i}", [128, 512], BF16) for i in range(2)]
        lrow = AR.alloc("lrow", [1, T], F32)
        rbc = AR.alloc("rbc", [64, T], BF16)
        csts = [AR.alloc(f"csts{i}", [128, 3 * NST], F32) for i in range(2)]
        cbf = [AR.alloc(f"cbf{i}", [128, 32], BF16) for i in range(2)]
        ctr2 = [AR.alloc(f"ctr2{i}", [16, 256], BF16) for i in range(2)]

        for nm, dst in [("wq", wq_h), ("wk", wk_h), ("wv", wv_h), ("wo", wo_h)]:
            kb.dma("pool", lambda e, d=dst, s=dr[nm]:
                   e.dma_start(out=d[:], in_=s.ap().rearrange("(kc p) o -> p kc o", p=128)),
                   writes=(B(nm + "b"),), stream="pre1")
        if _PHASES >= 7:
            precast_outer()

        kb.op("dve", lambda e: e.memset(VT[:, :, 64::65], 1.0),
              writes=(B("VT"),), small=True)

        def qkv(s_):
            for g in range(NKV):
                kb.op("dve", lambda e, g=g: e.memset(Kaug[g][64:66, :], 1.0),
                      writes=(B(f"kaug{g}"),), small=True)
            for c in range(T // TCA):
                t0 = c * TCA
                kb.dma("sp", lambda e, s_=s_, t0=t0: e.dma_start(
                    out=xch[:], in_=xin_chunk_ap(s_, t0, TCA)), writes=(B("xch"),))
                rms_norm_chunk(xch, B("xch"), gattn, B("c_gattn"), xn, B("xn"),
                               sq, TCA)
                for oc in range(DC):
                    pb = PSB[oc % 2]
                    proj_mm(xn, B("xn"), wq_h, B("wqb"), oc, TCA, pb, B(f"ps{oc % 2}"))
                    kb.op("act", lambda e, oc=oc, t0=t0, pb=pb,
                          sc=cv(float(HD) ** -0.5): e.activation(
                        out=Q[:, oc, t0:t0 + TCA], in_=pb[:, 0:TCA], func=AF.Copy,
                        scale=sc),
                        reads=(B(f"ps{oc % 2}"), cvB()), writes=(B("Q"),))
                for oc in range(2):
                    proj_mm(xn, B("xn"), wk_h, B("wkb"), oc, TCA, ps2, B("ps2"))
                    for half in range(2):
                        g = oc * 2 + half
                        kb.op("act", lambda e, g=g, half=half, t0=t0: e.activation(
                            out=Kaug[g][0:64, t0:t0 + TCA],
                            in_=ps2[half * 64:(half + 1) * 64, 0:TCA], func=AF.Copy),
                            reads=(B("ps2"),), writes=(B(f"kaug{g}"),))
                for oc in range(2):
                    proj_mm(xn, B("xn"), wv_h, B("wvb"), oc, TCA, ps2, B("ps2"))
                    kb.op("dve", lambda e, oc=oc: e.tensor_copy(
                        out=vtmp[:, oc, :], in_=ps2[:, 0:TCA]),
                        reads=(B("ps2"),), writes=(B("vtmp"),))
                for kc4 in range(TCA // 128):
                    for oc in range(2):
                        j = kc4 * 2 + oc
                        kb.transpose(psT[:, j * 128:(j + 1) * 128],
                                     vtmp[:, oc, kc4 * 128:(kc4 + 1) * 128],
                                     ident[:], reads=(B("vtmp"), B("c_ident")),
                                     psb=B("psT"))
                for kc4 in range(TCA // 128):
                    for oc in range(2):
                        j = kc4 * 2 + oc
                        kc = c * 4 + kc4
                        dst = VT[:, kc, oc * 130:(oc + 1) * 130].rearrange(
                            "p (g v) -> p g v", g=2)[:, :, 0:64]
                        kb.op("act", lambda e, j=j, dst=dst: e.activation(
                            out=dst, in_=psT[:, j * 128:(j + 1) * 128].rearrange(
                                "p (g v) -> p g v", g=2), func=AF.Copy),
                            reads=(B("psT"),), writes=(B("VT"),))

        def pass1_steps(s_, h):
            steps = []
            emit = steps.append
            return _pass1_build(s_, h, emit) or steps

        def _pass1_build(s_, h, emit):
            hp = (h % 2) * 64
            hc = h // 2
            g = h // NKV
            hh = h % 2
            qa, qaB = Qaug[hh], B(f"qaug{hh}")
            cst_, cstB = csts[hh], B(f"csts{hh}")
            def st_qcopy():
                kb.op("dve", lambda e, hp=hp, hc=hc, qa=qa: e.tensor_copy(
                    out=qa[0:64, :], in_=Q[hp:hp + 64, hc, :]),
                    reads=(B("Q"),), writes=(qaB,))
            emit(st_qcopy)
            def st_qk(qt, ci, kb0, kw):
                kb.mm_group([(ps2[:, 0:kw], qa[0:64, qt * 128:qt * 128 + 128],
                              Kaug[g][0:64, kb0:kb0 + kw])],
                            reads=(qaB, B(f"kaug{g}")), psb=B("ps2"))
                if ci == 0:
                    kb.op("dve", lambda e, qt=qt, kw=kw, cst_=cst_:
                          e.tensor_reduce(
                        out=cst_[:, qt:qt + 1], in_=ps2[:, 0:kw], axis=AX.X,
                        op=ALU.max),
                        reads=(B("ps2"),), writes=(cstB,), small=True)
                else:
                    kb.op("dve", lambda e, qt=qt, kw=kw, cst_=cst_:
                          e.tensor_reduce(
                        out=cst_[:, 16 + qt:17 + qt], in_=ps2[:, 0:kw],
                        axis=AX.X, op=ALU.max),
                        reads=(B("ps2"),), writes=(cstB,), small=True)
                    kb.op("dve", lambda e, qt=qt, cst_=cst_: e.tensor_max(
                        out=cst_[:, qt:qt + 1], in0=cst_[:, qt:qt + 1],
                        in1=cst_[:, 16 + qt:17 + qt]),
                        reads=(cstB,), writes=(cstB,), small=True)
            for qt in range(16):
                kmax = (qt + 1) * 128
                for ci, kb0 in enumerate(range(0, kmax, 512)):
                    kw = min(512, kmax - kb0)
                    emit(lambda qt=qt, ci=ci, kb0=kb0, kw=kw: st_qk(qt, ci, kb0, kw))

            def st_tail():
                _pass1_tail(h, cst_, cstB, qa, qaB)
            emit(st_tail)

        def _pass1_tail(h, cst_, cstB, qa, qaB):
            hh = h % 2
            kb.op("dve", lambda e, cst_=cst_: e.tensor_scalar_mul(
                out=cst_[:, 32:48], in0=cst_[:, 0:16], scalar1=-1.0),
                reads=(cstB,), writes=(cstB,), small=True)
            cb_ = cbf[hh]
            cbB = B(f"cbf{hh}")
            kb.op("dve", lambda e, cst_=cst_, cb_=cb_: e.tensor_copy(
                out=cb_[:, 0:16], in_=cst_[:, 32:48]),
                reads=(cstB,), writes=(cbB,), small=True)
            kb.op("dve", lambda e, cst_=cst_, cb_=cb_: e.tensor_sub(
                out=cst_[:, 32:48], in0=cst_[:, 32:48], in1=cb_[:, 0:16]),
                reads=(cstB, cbB), writes=(cstB,), small=True)
            kb.op("dve", lambda e, cst_=cst_, cb_=cb_: e.tensor_copy(
                out=cb_[:, 16:32], in_=cst_[:, 32:48]),
                reads=(cstB,), writes=(cbB,), small=True)
            ct_ = ctr2[hh]
            ctB = B(f"ctr2{hh}")
            kb.transpose(psT[0:16, 0:128], cb_[:, 0:16], ident[:],
                         reads=(cbB, B("c_ident")), psb=B("psT"))
            kb.op("act", lambda e, ct_=ct_: e.activation(
                out=ct_[0:16, 0:128], in_=psT[0:16, 0:128], func=AF.Copy),
                reads=(B("psT"),), writes=(ctB,))
            kb.transpose(psT[0:16, 128:256], cb_[:, 16:32], ident[:],
                         reads=(cbB, B("c_ident")), psb=B("psT"))
            kb.op("act", lambda e, ct_=ct_: e.activation(
                out=ct_[0:16, 128:256], in_=psT[0:16, 128:256], func=AF.Copy),
                reads=(B("psT"),), writes=(ctB,))
            kb.dma("pool", lambda e, qa=qa, ct_=ct_: e.dma_start(
                out=qa[64:65, :].rearrange("o (n c) -> o n c", n=16),
                in_=ct_[0:16, 0:128]),
                reads=(ctB,), writes=(qaB,))
            kb.dma("pool", lambda e, qa=qa, ct_=ct_: e.dma_start(
                out=qa[65:66, :].rearrange("o (n c) -> o n c", n=16),
                in_=ct_[0:16, 128:256]),
                reads=(ctB,), writes=(qaB,))

        def pass2_steps(s_, h):
            steps = []
            emit = steps.append
            hp = (h % 2) * 64
            hc = h // 2
            g = h // NKV
            hh = h % 2
            qa, qaB = Qaug[hh], B(f"qaug{hh}")
            def st_A(kc, qb, pi):
                qb0 = (kc * 128) // 512
                qlo = max(qb * 512, kc * 128)
                qhi = (qb + 1) * 512
                wdt = qhi - qlo
                stp = PSB[pi]
                sB = B(f"ps{pi}")
                ptt = pt[pi]
                ptB = B(f"pt{pi}")
                kb.mm_group([(stp[:, 0:wdt], Kaug[g][:, kc * 128:(kc + 1) * 128],
                              qa[:, qlo:qhi])],
                            reads=(B(f"kaug{g}"), qaB), psb=sB)
                if qb == qb0:
                    kb.op("dve", lambda e, stp=stp: e.tensor_add(
                        out=stp[:, 0:128], in0=stp[:, 0:128], in1=maskdt[:]),
                        reads=(B("c_maskdt"),), writes=(sB,))
                kb.op("act", lambda e, stp=stp, wdt=wdt, ptt=ptt: e.activation(
                    out=ptt[:, 0:wdt], in_=stp[:, 0:wdt], func=AF.Exp),
                    reads=(sB,), writes=(ptB,))

            def st_B(kc, qb, pi):
                qlo = max(qb * 512, kc * 128)
                qhi = (qb + 1) * 512
                wdt = qhi - qlo
                ptt = pt[pi]
                ptB = B(f"pt{pi}")
                kb.mm1(psS[0:65, qlo:qhi],
                       VT[:, kc, g * 65:(g + 1) * 65],
                       ptt[:, 0:wdt],
                       start=(kc == 0), stop=(kc == 4 * qb + 3),
                       reads=(B("VT"), ptB), psb=B("psS"))

            tiles = [(kc, qb) for kc in range(16)
                     for qb in range((kc * 128) // 512, 4)]
            emit(lambda: st_A(*tiles[0], 0))
            for i in range(len(tiles)):
                if i + 1 < len(tiles):
                    emit(lambda i=i: st_A(*tiles[i + 1], (i + 1) % 2))
                emit(lambda i=i: st_B(*tiles[i], i % 2))
            emit(lambda: _pass2_tail(hp, hc))
            return steps

        def _pass2_tail(hp, hc):
            kb.op("dve", lambda e: e.tensor_copy(out=lrow[0:1, :],
                                                 in_=psS[64:65, 0:T]),
                  reads=(B("psS"),), writes=(B("lrow"),), small=True)
            kb.op("dve", lambda e: e.reciprocal(out=lrow[0:1, :], in_=lrow[0:1, :]),
                  reads=(B("lrow"),), writes=(B("lrow"),), small=True)
            for blk in range(4):
                kb.mm_group([(psS[64:128, blk * 512:(blk + 1) * 512],
                              ones_r32[0:1, 0:64],
                              lrow[0:1, blk * 512:(blk + 1) * 512])],
                            reads=(B("lrow"), B("ones")), psb=B("psS"))
                kb.op("act", lambda e, blk=blk: e.activation(
                    out=rbc[:, blk * 512:(blk + 1) * 512],
                    in_=psS[64:128, blk * 512:(blk + 1) * 512],
                    func=AF.Copy), reads=(B("psS"),), writes=(B("rbc"),))
            for blk in range(4):
                kb.op("dve", lambda e, hp=hp, hc=hc, blk=blk: e.tensor_tensor(
                    out=Ofm[hp:hp + 64, hc, blk * 512:(blk + 1) * 512],
                    in0=psS[0:64, blk * 512:(blk + 1) * 512],
                    in1=rbc[:, blk * 512:(blk + 1) * 512], op=ALU.mult),
                    reads=(B("psS"), B("rbc")), writes=(B("Ofm"),))

        def wo_resid(s_):
            for c in range(T // TCA):
                t0 = c * TCA
                kb.dma("sp", lambda e, s_=s_, t0=t0: e.dma_start(
                    out=xch[:], in_=xin_chunk_ap(s_, t0, TCA)), writes=(B("xch"),))
                for oc in range(DC):
                    pb = PSB[oc % 2]
                    mm = [(pb[:, 0:TCA], wo_h[:, kc, oc * 128:(oc + 1) * 128],
                           Ofm[:, kc, t0:t0 + TCA]) for kc in range(DC)]
                    kb.mm_group(mm, reads=(B("Ofm"), B("wob")), psb=B(f"ps{oc % 2}"))
                    kb.op("dve", lambda e, oc=oc, pb=pb: e.tensor_add(
                        out=xch[:, oc, :], in0=xch[:, oc, :], in1=pb[:, 0:TCA]),
                        reads=(B(f"ps{oc % 2}"),), writes=(B("xch"),))
                kb.dma("sp", lambda e, s_=s_, t0=t0: e.dma_start(
                    out=xs_chunk_ap(xs_a, s_, t0, TCA), in_=xch[:]),
                    reads=(B("xch"),), writes=(B("xs_a"),))

        for s_ in range(NSEQ):
            qkv(s_)
            for st in pass1_steps(s_, 0):
                st()
            for h in range(NH):
                s1 = pass1_steps(s_, h + 1) if h + 1 < NH else []
                s2 = pass2_steps(s_, h)
                i = j = 0
                n1, n2 = len(s1), len(s2)
                while i < n2 or j < n1:
                    if i < n2:
                        s2[i]()
                        i += 1
                    while j < n1 and j * n2 <= i * n1:
                        s1[j]()
                        j += 1
            wo_resid(s_)


    def phase_mamba_pipe(nl, src0_t, src0B):
        AR.reset()
        TCE = TCM + 1
        NB = NST * TCM
        NU = 16 * nl
        NCH = T // TCM
        pSv = [psS[:, 512 * b:512 * (b + 1)] for b in range(4)]
        pSB = [B(f"psSb{b}") for b in range(4)]

        win_sb = [AR.alloc(f"mwin{i}", [128, DC, 128], BF16) for i in range(2)]
        wx_sb = [AR.alloc(f"mwx{i}", [128, DC, 128], BF16) for i in range(2)]
        wo_sb = [AR.alloc(f"mwo{i}", [128, DC, 128], BF16) for i in range(2)]
        xch = [AR.alloc(f"mxch{i}", [128, DC, TCM], F32) for i in range(3)]
        sz = [AR.alloc(f"msz{i}", [128, DC, TCM], BF16) for i in range(3)]
        xn = [AR.alloc(f"mxn{i}", [128, DC, TCM], BF16) for i in range(2)]
        xbh = [AR.alloc(f"mxbh{i}", [128, DC, KCONV - 1 + TCM], BF16)
               for i in range(2)]
        s2 = [AR.alloc(f"ms2{i}", [128, DC, TCM], BF16) for i in range(2)]
        xcb = [AR.alloc(f"mxcb{i}", [128, DC, TCM], BF16) for i in range(2)]
        delta = [AR.alloc(f"mdelta{i}", [128, DC, TCM], BF16) for i in range(2)]
        dxcb = [AR.alloc(f"mdxcb{i}", [128, DC, TCM], BF16) for i in range(2)]
        Bb = [AR.alloc(f"mBb{i}", [128, NST, TCM], BF16) for i in range(2)]
        Cb = [AR.alloc(f"mCb{i}", [128, NST, TCM], BF16) for i in range(2)]
        bcr = [AR.alloc(f"mbcr{i}", [32, TCM], BF16) for i in range(2)]
        abar = [AR.alloc(f"mabar{i}", [128, NST, TCE], BF16) for i in range(3)]
        bbym = [AR.alloc(f"mbby{i}", [128, NST, TCE], BF16) for i in range(3)]
        carry2 = [AR.alloc(f"mcarry{i}", [128, DC, NST], F32) for i in range(2)]
        ybuf = AR.alloc("mybuf", [128, DC, TCM], BF16)
        gb = [AR.alloc(f"mgb{i}", [128, DC, TCM], BF16) for i in range(2)]
        for i in range(3):
            kb.op("dve", lambda e, i=i: e.memset(abar[i][:, :, 0:1], 0.0),
                  writes=(B(f"mabar{i}"),), small=True)

        srcs = [(src0_t, src0B), (xs_a, B("xs_a")), (xs_b, B("xs_b"))]
        dsts = [(xs_a, B("xs_a")), (xs_b, B("xs_b")), (xs_a, B("xs_a"))]

        for l in range(nl):
            for src_nm, dst_t, noc in ((f"win{l}", winb[l], 16),
                                       (f"wx{l}", wxb[l], 9),
                                       (f"wout{l}", woutb[l], DC)):
                srcv = dr[src_nm].ap().rearrange("(kc p) (oc o) -> oc p kc o",
                                                 p=128, o=128)
                for oc in range(noc):
                    kb.dma("pool", lambda e, oc=oc, dst_t=dst_t, srcv=srcv:
                           e.dma_start(
                        out=dst_t.ap()[oc].rearrange("p (kc o) -> p kc o", o=128),
                        in_=srcv[oc]), writes=(B(dst_t.name),), stream=f"prem{l}")

        def stream_w(cache_t, oc, slot, slotB):
            kb.dma("sp", lambda e, oc=oc, w=slot, cache_t=cache_t: e.dma_start(
                out=w[:], in_=cache_t.ap()[oc].rearrange("p (kc o) -> p kc o", o=128)),
                reads=(B(cache_t.name),), writes=(slotB,))

        def U(u):
            return u // 16, (u % 16) % NSEQ, (u % 16) // NSEQ

        def st_bcast(u):
            p2 = u % 2
            for r0, r1, dstt, dstB, rowB, ei in (
                    (0, 16, Bb[p2], B(f"mBb{p2}"), B(f"mBbr{p2}"), "dve"),
                    (16, 32, Cb[p2], B(f"mCb{p2}"), B(f"mCbr{p2}"), "act")):
                kb.dma("sp", lambda e, rows=bcr[p2][r0:r1, :], dstt=dstt:
                       e.dma_start(out=dstt[0:1, :, :], in_=rows),
                       reads=(B(f"mbcr{p2}"),), writes=(rowB, dstB))
                flat = dstt[:].rearrange("p n t -> p (n t)")
                for j in range(NB // 512):
                    pb, pbB = PSB[j % 2], B(f"ps{j % 2}")
                    kb.mm_group([(pb[:, 0:512], ones_r[:],
                                  flat[0:1, j * 512:(j + 1) * 512])],
                                reads=(rowB, B("ones")), psb=pbB)
                    if ei == "dve":
                        kb.op("dve", lambda e, j=j, pb=pb, flat=flat: e.tensor_copy(
                            out=flat[:, j * 512:(j + 1) * 512], in_=pb[:, 0:512]),
                            reads=(pbB,), writes=(dstB,))
                    else:
                        kb.op("act", lambda e, j=j, pb=pb, flat=flat: e.activation(
                            out=flat[:, j * 512:(j + 1) * 512], in_=pb[:, 0:512],
                            func=AF.Copy),
                            reads=(pbB,), writes=(dstB,))

        def st_xch_dma(u):
            l, s_, c = U(u)
            src_t, srcB = srcs[l]
            kb.dma("sp", lambda e, xchp=xch[u % 3], src_t=src_t, s_=s_, c=c:
                   e.dma_start(out=xchp[:],
                               in_=xs_chunk_ap(src_t, s_, c * TCM, TCM)),
                   reads=(srcB,), writes=(B(f"mxch{u % 3}"),))

        def st_conv(u):
            l, s_, c = U(u)
            p2 = u % 2
            convt, convtB = xn[p2], B(f"mxn{p2}")
            xbhp, xbhB = xbh[s_], B(f"mxbh{s_}")
            s2p, s2B = s2[s_], B(f"ms2{s_}")
            kb.op("dve", lambda e, s2p=s2p, xbhp=xbhp: e.tensor_tensor(
                out=s2p[:], in0=xbhp[:, :, 0:TCM],
                in1=cw[l][:, :, 0:1].broadcast_to((128, DC, TCM)),
                op=ALU.mult),
                reads=(xbhB, B(f"c_cw{l}")), writes=(s2B,))
            for k in range(1, KCONV):
                kb.op("dve", lambda e, k=k, xbhp=xbhp: e.tensor_tensor(
                    out=convt[:], in0=xbhp[:, :, k:k + TCM],
                    in1=cw[l][:, :, k:k + 1].broadcast_to((128, DC, TCM)),
                    op=ALU.mult),
                    reads=(xbhB, B(f"c_cw{l}")), writes=(convtB,))
                kb.op("dve", lambda e, s2p=s2p: e.tensor_add(
                    out=s2p[:], in0=s2p[:], in1=convt[:]),
                    reads=(convtB, s2B), writes=(s2B,))
            kb.op("dve", lambda e, s2p=s2p: e.tensor_tensor(
                out=s2p[:], in0=s2p[:],
                in1=cb[l][:].unsqueeze(2).broadcast_to((128, DC, TCM)),
                op=ALU.add),
                reads=(B(f"c_cb{l}"), s2B), writes=(s2B,))

        def st_rms_actpe(u):
            l = u // 16
            p2 = u % 2
            xchp, xchB = xch[u % 3], B(f"mxch{u % 3}")
            xnp, xnB = xn[p2], B(f"mxn{p2}")
            kb.op("act", lambda e, xnp=xnp, xchp=xchp: e.activation(
                out=xnp[:], in_=xchp[:], func=AF.Square),
                reads=(xchB,), writes=(xnB,))
            kb.mm_group([(pSv[0][0:1, 0:TCM], ones_c[:], xnp[:, kc, :])
                         for kc in range(DC)],
                        reads=(xnB, B("ones")), psb=pSB[0])
            kb.op("act", lambda e, sc=cv(1.0 / D)[0:1, :], bi=cv(1e-6)[0:1, :]:
                  e.activation(out=rowst[0:1, 0:TCM], in_=pSv[0][0:1, 0:TCM],
                               func=AF.Ln, scale=sc, bias=bi),
                  reads=(pSB[0], cvB()), writes=(B("rowst"),), small=True)
            kb.op("act", lambda e, sc=cv(-0.5)[0:1, :]: e.activation(
                out=rowst[0:1, TCM:2 * TCM], in_=rowst[0:1, 0:TCM],
                func=AF.Exp, scale=sc),
                reads=(B("rowst"), cvB()), writes=(B("rowst"),), small=True)
            kb.mm_group([(pSv[0][:, 0:TCM], ones_r32[:],
                          rowst[0:1, TCM:2 * TCM])],
                        reads=(B("rowst"), B("ones")), psb=pSB[0])

        def st_rms_stt(u):
            l, s_, c = U(u)
            p2 = u % 2
            xchp, xchB = xch[u % 3], B(f"mxch{u % 3}")
            xnp, xnB = xn[p2], B(f"mxn{p2}")
            for dc in range(DC):
                kb.op("dve", lambda e, dc=dc, xnp=xnp, xchp=xchp:
                      e.scalar_tensor_tensor(
                    out=xnp[:, dc, :], in0=xchp[:, dc, :],
                    scalar=gm[l][:, dc:dc + 1], in1=pSv[0][:, 0:TCM],
                    op0=ALU.mult, op1=ALU.mult),
                    reads=(xchB, pSB[0], B(f"c_gm{l}")), writes=(xnB,))
            xbhp, xbhB = xbh[s_], B(f"mxbh{s_}")
            if c == 0:
                kb.op("dve", lambda e, xbhp=xbhp: e.memset(
                    xbhp[:, :, 0:KCONV - 1], 0.0),
                    writes=(xbhB,), small=True)
            else:
                kb.op("dve", lambda e, xbhp=xbhp: e.tensor_copy(
                    out=xbhp[:, :, 0:KCONV - 1],
                    in_=xbhp[:, :, TCM:TCM + KCONV - 1]),
                    reads=(xbhB,), writes=(xbhB,), small=True)

        def st_scan(u):
            l, s_, c = U(u)
            p2 = u % 2
            dlB, dxB = B(f"mdelta{p2}"), B(f"mdxcb{p2}")
            carB = B(f"mcarry{s_}")
            for dc in range(DC):
                pi = dc % 3
                ab, abB = abar[pi], B(f"mabar{pi}")
                bby, bbyB = bbym[pi], B(f"mbby{pi}")
                for n in range(NST):
                    kb.op("act", lambda e, dc=dc, n=n, ab=ab: e.activation(
                        out=ab[:, n, 1:], in_=delta[p2][:, dc, :], func=AF.Exp,
                        scale=amat[l][:, dc, n:n + 1]),
                        reads=(dlB, B(f"amat{l}")), writes=(abB,))
                kb.op("pool", lambda e, dc=dc, bby=bby: e.tensor_tensor(
                    out=bby[:, :, 1:], in0=Bb[p2][:],
                    in1=dxcb[p2][:, dc, :].unsqueeze(1).broadcast_to(
                        (128, NST, TCM)), op=ALU.mult),
                    reads=(B(f"mBb{p2}"), dxB), writes=(bbyB,))
                if c == 0:
                    kb.op("dve", lambda e, bby=bby: e.memset(bby[:, :, 0:1], 0.0),
                          writes=(bbyB,), small=True)
                else:
                    kb.op("dve", lambda e, dc=dc, bby=bby: e.tensor_copy(
                        out=bby[:, :, 0:1], in_=carry2[s_][:, dc, :].unsqueeze(2)),
                        reads=(carB,), writes=(bbyB,), small=True)
                kb.op("dve", lambda e, ab=ab, bby=bby: e.tensor_tensor_scan(
                    out=ab[:].rearrange("p n t -> p (n t)"),
                    data0=ab[:].rearrange("p n t -> p (n t)"),
                    data1=bby[:].rearrange("p n t -> p (n t)"),
                    initial=0.0, op0=ALU.mult, op1=ALU.add),
                    reads=(abB, bbyB), writes=(abB,))
                if c < NCH - 1:
                    kb.op("dve", lambda e, dc=dc, ab=ab: e.tensor_copy(
                        out=carry2[s_][:, dc, :], in_=ab[:, :, TCM:TCE].squeeze(2)),
                        reads=(abB,), writes=(carB,), small=True)
                kb.op("dve", lambda e, ab=ab: e.memset(ab[:, :, 0:1], 0.0),
                      writes=(abB,), small=True)
                kb.op("dve", lambda e, ab=ab, bby=bby: e.tensor_tensor(
                    out=bby[:, :, 1:], in0=ab[:, :, 1:], in1=Cb[p2][:],
                    op=ALU.mult),
                    reads=(abB, B(f"mCb{p2}")), writes=(bbyB,))
                for lvl in (8, 4, 2):
                    kb.op("pool", lambda e, lvl=lvl, bby=bby: e.tensor_tensor(
                        out=bby[:, 0:lvl, 1:], in0=bby[:, 0:lvl, 1:],
                        in1=bby[:, lvl:2 * lvl, 1:], op=ALU.add),
                        reads=(bbyB,), writes=(bbyB,))
                kb.op("pool", lambda e, dc=dc, bby=bby: e.tensor_tensor(
                    out=ybuf[:, dc, :], in0=bby[:, 0:1, 1:].squeeze(1),
                    in1=bby[:, 1:2, 1:].squeeze(1), op=ALU.add),
                    reads=(bbyB,), writes=(B("mybuf"),))

        def st_xz(u):
            l, s_, c = U(u)
            p2, p3 = u % 2, u % 3
            xbhp, xbhB = xbh[s_], B(f"mxbh{s_}")
            for oc in range(16):
                stream_w(winb[l], oc, win_sb[oc % 2], B(f"mwin{oc % 2}"))
                pb, pbB = pSv[2 + oc % 2], pSB[2 + oc % 2]
                kb.mm_group([(pb[:, 0:TCM], win_sb[oc % 2][:, kc, :],
                              xn[p2][:, kc, :]) for kc in range(DC)],
                            reads=(B(f"mxn{p2}"), B(f"mwin{oc % 2}")), psb=pbB)
                if oc < 8:
                    kb.op("act", lambda e, oc=oc, pb=pb, xbhp=xbhp: e.activation(
                        out=xbhp[:, oc, KCONV - 1:], in_=pb[:, 0:TCM],
                        func=AF.Copy),
                        reads=(pbB,), writes=(xbhB,))
                else:
                    kb.op("act", lambda e, oc=oc, pb=pb, szp=sz[p3]: e.activation(
                        out=szp[:, oc - 8, :], in_=pb[:, 0:TCM], func=AF.Silu),
                        reads=(pbB,), writes=(B(f"msz{p3}"),))

        def st_wx(u):
            l, s_, c = U(u)
            p2 = u % 2
            s2p, s2B = s2[s_], B(f"ms2{s_}")
            dlB = B(f"mdelta{p2}")
            kb.op("act", lambda e, xcbp=xcb[p2], s2p=s2p: e.activation(
                out=xcbp[:], in_=s2p[:], func=AF.Silu),
                reads=(s2B,), writes=(B(f"mxcb{p2}"),))
            wxps = [(ps2, B("ps2")), (pSv[1], pSB[1])]
            for oc in range(9):
                stream_w(wxb[l], oc, wx_sb[oc % 2], B(f"mwx{oc % 2}"))
                pb, pbB = wxps[oc % 2]
                kb.mm_group([(pb[:, 0:TCM], wx_sb[oc % 2][:, kc, :],
                              xcb[p2][:, kc, :]) for kc in range(DC)],
                            reads=(B(f"mxcb{p2}"), B(f"mwx{oc % 2}")), psb=pbB)
                if oc < 8:
                    kb.op("act", lambda e, oc=oc, pb=pb, s2p=s2p: e.activation(
                        out=s2p[:, oc, :], in_=pb[:, 0:TCM], func=AF.Abs),
                        reads=(pbB,), writes=(s2B,), small=True)
                    kb.op("act", lambda e, oc=oc, pb=pb, dp_=delta[p2]:
                          e.activation(
                        out=dp_[:, oc, :], in_=pb[:, 0:TCM], func=AF.Relu),
                        reads=(pbB,), writes=(dlB,), small=True)
                    kb.op("act", lambda e, oc=oc, sc=cv(-1.0), s2p=s2p:
                          e.activation(
                        out=s2p[:, oc, :], in_=s2p[:, oc, :], func=AF.Exp,
                        scale=sc),
                        reads=(s2B, cvB()), writes=(s2B,), small=True)
                    kb.op("act", lambda e, oc=oc, bi=cv(1.0), s2p=s2p:
                          e.activation(
                        out=s2p[:, oc, :], in_=s2p[:, oc, :], func=AF.Ln,
                        bias=bi),
                        reads=(s2B, cvB()), writes=(s2B,), small=True)
                    kb.op("pool", lambda e, oc=oc, dp_=delta[p2], s2p=s2p:
                          e.tensor_tensor(
                        out=dp_[:, oc, :], in0=dp_[:, oc, :], in1=s2p[:, oc, :],
                        op=ALU.add),
                        reads=(s2B, dlB), writes=(dlB,), small=True)
                else:
                    kb.op("dve", lambda e, pb=pb: e.tensor_copy(
                        out=bcr[p2][:], in_=pb[0:32, 0:TCM]),
                        reads=(pbB,), writes=(B(f"mbcr{p2}"),), small=True)

        def st_dxcb(u):
            p2 = u % 2
            kb.op("dve", lambda e, dxp=dxcb[p2]: e.tensor_tensor(
                out=dxp[:], in0=delta[p2][:], in1=xcb[p2][:], op=ALU.mult),
                reads=(B(f"mdelta{p2}"), B(f"mxcb{p2}")),
                writes=(B(f"mdxcb{p2}"),))

        def st_gate(u):
            l, s_, c = U(u)
            p2, p3 = u % 2, u % 3
            gbp, gbB = gb[p2], B(f"mgb{p2}")
            kb.op("dve", lambda e, gbp=gbp: e.tensor_tensor(
                out=gbp[:], in0=xcb[p2][:],
                in1=dp[l][:].unsqueeze(2).broadcast_to((128, DC, TCM)),
                op=ALU.mult),
                reads=(B(f"mxcb{p2}"), B(f"c_dp{l}")), writes=(gbB,))
            kb.op("dve", lambda e, gbp=gbp: e.tensor_add(
                out=gbp[:], in0=gbp[:], in1=ybuf[:]),
                reads=(B("mybuf"), gbB), writes=(gbB,))
            kb.op("dve", lambda e, gbp=gbp, szp=sz[p3]: e.tensor_mul(
                out=gbp[:], in0=gbp[:], in1=szp[:]),
                reads=(B(f"msz{p3}"), gbB), writes=(gbB,))

        def st_wout(u):
            l, s_, c = U(u)
            p2, p3 = u % 2, u % 3
            xchp, xchB = xch[p3], B(f"mxch{p3}")
            wps = [(pSv[0], pSB[0]), (ps2, B("ps2"))]
            for oc in range(DC):
                stream_w(woutb[l], oc, wo_sb[oc % 2], B(f"mwo{oc % 2}"))
                pb, pbB = wps[oc % 2]
                kb.mm_group([(pb[:, 0:TCM], wo_sb[oc % 2][:, kc, :],
                              gb[p2][:, kc, :]) for kc in range(DC)],
                            reads=(B(f"mgb{p2}"), B(f"mwo{oc % 2}")), psb=pbB)
                kb.op("dve", lambda e, oc=oc, pb=pb, xchp=xchp: e.tensor_add(
                    out=xchp[:, oc, :], in0=xchp[:, oc, :], in1=pb[:, 0:TCM]),
                    reads=(pbB, xchB), writes=(xchB,))
            dst_t, dstB = dsts[l]
            kb.dma("sp", lambda e, xchp=xchp, dst_t=dst_t, s_=s_, c=c:
                   e.dma_start(out=xs_chunk_ap(dst_t, s_, c * TCM, TCM),
                               in_=xchp[:]),
                   reads=(xchB,), writes=(dstB,))

        for r in range(NU + 2):
            uA, uB, uC = r - 2, r - 1, r
            if 0 <= uA < NU:
                st_bcast(uA)
            if uC < NU:
                st_xch_dma(uC)
            if 0 <= uB < NU:
                st_conv(uB)
            if uC < NU:
                st_rms_actpe(uC)
                st_rms_stt(uC)
            if 0 <= uA < NU:
                st_scan(uA)
            if uC < NU:
                st_xz(uC)
            if 0 <= uB < NU:
                st_wx(uB)
                st_dxcb(uB)
            if 0 <= uA < NU:
                st_gate(uA)
                st_wout(uA)

    def phase7(xs_src, srcB):
        AR.reset()
        w1sb = [AR.alloc(f"w1sb{i}", [128, DC, 128], BF16) for i in range(2)]
        w2sb = [AR.alloc(f"w2sb{i}", [128, DC, 128], BF16) for i in range(2)]
        hsb = [AR.alloc(f"hsb{i}", [128, OC_H, 128], BF16) for i in range(2)]
        xln = AR.alloc("xln", [128, DC, TCF], F32)
        xlnb = AR.alloc("xlnb", [128, DC, TCF], BF16)
        t1 = AR.alloc("t1o", [128, OC_H, TCF], BF16)
        t2 = AR.alloc("t2o", [128, OC_H, TCF], BF16)
        vsum = AR.alloc("vsum", [128, DC, TCF], F32)
        vb = AR.alloc("vbo", [128, DC, TCF], BF16)
        sqo = AR.alloc("sqo", [128, DC, TCF], BF16)
        sgcol = AR.alloc("sgcol", [128, TCF], F32)
        yout = AR.alloc("yout", [128, DC, TCF], F32)

        psl = [psS[:, b * 512:(b + 1) * 512] for b in range(4)]
        pslB = [B(f"psS{b}7") for b in range(4)]
        for b in pslB:
            b.psum = True

        def precast(src_dr, dst_dr, noc, nkc):
            srcv = src_dr.ap().rearrange("(kc p) (oc o) -> oc p kc o", p=128, o=128)
            nm = dst_dr.name
            for oc in range(noc):
                if nkc <= 16:
                    kb.dma("pool", lambda e, oc=oc: e.dma_start(
                        out=dst_dr.ap()[oc].rearrange("p (kc o) -> p kc o", o=128),
                        in_=srcv[oc]), writes=(B(nm),), stream="pre7")
                else:
                    for hh in range(2):
                        k0, k1 = hh * nkc // 2, (hh + 1) * nkc // 2
                        kb.dma("pool", lambda e, oc=oc, k0=k0, k1=k1: e.dma_start(
                            out=dst_dr.ap()[oc].rearrange(
                                "p (kc o) -> p kc o", o=128)[:, k0:k1, :],
                            in_=srcv[oc][:, k0:k1, :]), writes=(B(nm),),
                            stream="pre7")
        def ln_rows(v_h, vB, eps):
            kb.op("pool", lambda e: e.tensor_copy(out=vb[:], in_=v_h[:]),
                  reads=(vB,), writes=(B("vbo"),))
            kb.op("act", lambda e: e.activation(out=sqo[:], in_=v_h[:],
                                                func=AF.Square),
                  reads=(vB,), writes=(B("sqo"),))
            kb.mm_group([(psl[0][0:1, 0:TCF], ones_c[:], vb[:, kc, :])
                         for kc in range(DC)],
                        reads=(B("vbo"), B("ones")), psb=pslB[0])
            kb.mm_group([(psl[1][0:1, 0:TCF], ones_c[:], sqo[:, kc, :])
                         for kc in range(DC)],
                        reads=(B("sqo"), B("ones")), psb=pslB[1])
            r0 = rowst[0:1, 0:TCF]
            r1 = rowst[0:1, TCF:2 * TCF]
            r2 = rowst[0:1, 2 * TCF:3 * TCF]
            kb.op("dve", lambda e: e.tensor_scalar_mul(out=r0, in0=psl[0][0:1, 0:TCF],
                                                       scalar1=1.0 / D),
                  reads=(pslB[0],), writes=(B("rowst"),), small=True)
            kb.op("dve", lambda e: e.tensor_scalar_mul(out=r1, in0=psl[1][0:1, 0:TCF],
                                                       scalar1=1.0 / D),
                  reads=(pslB[1],), writes=(B("rowst"),), small=True)
            kb.op("dve", lambda e: e.tensor_tensor(out=r2, in0=r0, in1=r0,
                                                   op=ALU.mult),
                  reads=(B("rowst"),), writes=(B("rowst"),), small=True)
            kb.op("dve", lambda e: e.tensor_sub(out=r1, in0=r1, in1=r2),
                  reads=(B("rowst"),), writes=(B("rowst"),), small=True)
            kb.op("act", lambda e, bi=cv(eps)[0:1, :]: e.activation(
                out=r2, in_=r1, func=AF.Ln, bias=bi),
                  reads=(B("rowst"), cvB()), writes=(B("rowst"),), small=True)
            kb.op("act", lambda e, sc=cv(-0.5)[0:1, :]: e.activation(
                out=r1, in_=r2, func=AF.Exp, scale=sc),
                  reads=(B("rowst"), cvB()), writes=(B("rowst"),), small=True)
            kb.op("dve", lambda e: e.tensor_tensor(out=r2, in0=r0, in1=r1,
                                                   op=ALU.mult),
                  reads=(B("rowst"),), writes=(B("rowst"),), small=True)
            kb.op("dve", lambda e: e.tensor_scalar_mul(out=r2, in0=r2, scalar1=-1.0),
                  reads=(B("rowst"),), writes=(B("rowst"),), small=True)
            kb.mm_group([(psl[2][:, 0:TCF], ones_r32[:], r1)],
                        reads=(B("rowst"), B("ones")), psb=pslB[2])
            kb.mm_group([(psl[3][:, 0:TCF], ones_r32[:], r2)],
                        reads=(B("rowst"), B("ones")), psb=pslB[3])

        def ln_apply(v_h, vB, out_h, outB):
            for dc in range(DC):
                kb.op("dve", lambda e, dc=dc: e.tensor_tensor(
                    out=out_h[:, dc, :], in0=v_h[:, dc, :], in1=psl[2][:, 0:TCF],
                    op=ALU.mult), reads=(vB, pslB[2]), writes=(outB,))
            for dc in range(DC):
                kb.op("dve", lambda e, dc=dc: e.tensor_add(
                    out=out_h[:, dc, :], in0=out_h[:, dc, :], in1=psl[3][:, 0:TCF]),
                    reads=(pslB[3],), writes=(outB,))

        NCH = (NSEQ * T) // TCF
        chl = range(NCH) if _P7CH is None else list(_P7CH)
        for ch in chl:
            s_ = ch // (T // TCF)
            t0 = (ch % (T // TCF)) * TCF
            kb.dma("sp", lambda e, s_=s_, t0=t0: e.dma_start(
                out=vsum[:], in_=xin_chunk_ap(s_, t0, TCF)), writes=(B("vsum"),))
            kb.dma("sp", lambda e, s_=s_, t0=t0: e.dma_start(
                out=yout[:], in_=xs_chunk_ap(xs_src, s_, t0, TCF)),
                reads=(srcB,), writes=(B("yout"),))
            kb.op("dve", lambda e: e.tensor_add(out=vsum[:], in0=vsum[:],
                                                in1=yout[:]),
                  reads=(B("yout"),), writes=(B("vsum"),))
            ln_rows(vsum, B("vsum"), 1e-5)
            ln_apply(vsum, B("vsum"), xln, B("xln"))
            kb.op("pool", lambda e: e.tensor_copy(out=xlnb[:], in_=xln[:]),
                  reads=(B("xln"),), writes=(B("xlnb"),))
            for oc in range(OC_H):
                kb.dma("sp", lambda e, oc=oc, w=w1sb[oc % 2]: e.dma_start(
                    out=w[:], in_=w1b.ap()[oc].rearrange("p (kc o) -> p kc o", o=128)),
                    reads=(B("w1b"),), writes=(B(f"w1sb{oc % 2}"),))
                kb.dma("sp", lambda e, oc=oc, w=w2sb[oc % 2]: e.dma_start(
                    out=w[:], in_=w2b.ap()[oc].rearrange("p (kc o) -> p kc o", o=128)),
                    reads=(B("w2b"),), writes=(B(f"w2sb{oc % 2}"),))
                pg = PSB[(2 * oc) % 3]
                pgB = B(f"ps{(2 * oc) % 3}")
                pu = PSB[(2 * oc + 1) % 3]
                puB = B(f"ps{(2 * oc + 1) % 3}")
                kb.mm_group([(pg[:, 0:TCF], w1sb[oc % 2][:, kc, :], xlnb[:, kc, :])
                             for kc in range(DC)],
                            reads=(B("xlnb"), B(f"w1sb{oc % 2}")), psb=pgB)
                kb.op("act", lambda e, oc=oc, pg=pg: e.activation(
                    out=sgcol[:], in_=pg[:, 0:TCF], func=AF.Silu,
                    scale=s1v[:, oc:oc + 1]),
                    reads=(pgB, B("c_s1")), writes=(B("sgcol"),))
                kb.mm_group([(pu[:, 0:TCF], w2sb[oc % 2][:, kc, :], xlnb[:, kc, :])
                             for kc in range(DC)],
                            reads=(B("xlnb"), B(f"w2sb{oc % 2}")), psb=puB)
                kb.op("dve", lambda e, oc=oc, pu=pu: e.scalar_tensor_tensor(
                    out=t1[:, oc, :], in0=pu[:, 0:TCF], scalar=s2v[:, oc:oc + 1],
                    in1=sgcol[:], op0=ALU.mult, op1=ALU.mult),
                    reads=(puB, B("c_s2"), B("sgcol")), writes=(B("t1o"),))
            for oc in range(OC_H):
                kb.dma("sp", lambda e, oc=oc, w=hsb[oc % 2]: e.dma_start(
                    out=w[:], in_=h1b.ap()[oc].rearrange("p (kc o) -> p kc o", o=128)),
                    reads=(B("h1b"),), writes=(B(f"hsb{oc % 2}"),))
                pb = PSB[oc % 3]
                pbB = B(f"ps{oc % 3}")
                kb.mm_group([(pb[:, 0:TCF], hsb[oc % 2][:, kc, :], t1[:, kc, :])
                             for kc in range(OC_H)],
                            reads=(B("t1o"), B(f"hsb{oc % 2}")), psb=pbB)
                kb.op("act", lambda e, oc=oc, pb=pb: e.activation(
                    out=t2[:, oc, :], in_=pb[:, 0:TCF], func=AF.Gelu,
                    scale=sh1v[:, oc:oc + 1]),
                    reads=(pbB, B("c_sh1")), writes=(B("t2o"),))
            for oc in range(OC_H):
                kb.dma("sp", lambda e, oc=oc, w=hsb[oc % 2]: e.dma_start(
                    out=w[:], in_=h2b.ap()[oc].rearrange("p (kc o) -> p kc o", o=128)),
                    reads=(B("h2b"),), writes=(B(f"hsb{oc % 2}"),))
                pb = PSB[oc % 3]
                pbB = B(f"ps{oc % 3}")
                kb.mm_group([(pb[:, 0:TCF], hsb[oc % 2][:, kc, :], t2[:, kc, :])
                             for kc in range(OC_H)],
                            reads=(B("t2o"), B(f"hsb{oc % 2}")), psb=pbB)
                kb.op("act", lambda e, oc=oc, pb=pb: e.activation(
                    out=t1[:, oc, :], in_=pb[:, 0:TCF], func=AF.Gelu,
                    scale=sh2v[:, oc:oc + 1]),
                    reads=(pbB, B("c_sh2")), writes=(B("t1o"),))
            for oc in range(DC):
                kb.dma("sp", lambda e, oc=oc, w=hsb[oc % 2]: e.dma_start(
                    out=w[:], in_=w3b.ap()[oc].rearrange("p (kc o) -> p kc o", o=128)),
                    reads=(B("w3b"),), writes=(B(f"hsb{oc % 2}"),))
                pb = PSB[oc % 3]
                pbB = B(f"ps{oc % 3}")
                kb.mm_group([(pb[:, 0:TCF], hsb[oc % 2][:, kc, :], t1[:, kc, :])
                             for kc in range(OC_H)],
                            reads=(B("t1o"), B(f"hsb{oc % 2}")), psb=pbB)
                kb.op("dve", lambda e, oc=oc, pb=pb: e.scalar_tensor_tensor(
                    out=vsum[:, oc, :], in0=pb[:, 0:TCF], scalar=s3v[:, oc:oc + 1],
                    in1=xln[:, oc, :], op0=ALU.mult, op1=ALU.add),
                    reads=(pbB, B("c_s3"), B("xln")), writes=(B("vsum"),))
            if _DUMP and ch == 2:
                for nm, h, shp, dt_ in [
                    ("uA", vsum, [128, DC, TCF], F32),
                    ("xlnA", xln, [128, DC, TCF], F32),
                    ("t3A", t1, [128, OC_H, TCF], BF16),
                    ("t2A", t2, [128, OC_H, TCF], BF16),
                ]:
                    dtv = nc.dram_tensor("dbg7_" + nm, shp, dt_, kind="ExternalOutput")
                    kb.dma("sp", lambda e, d=dtv, src=h: e.dma_start(
                        out=d.ap(), in_=src[:]), reads=(B("vsum"), B("xln"),
                        B("t1o"), B("t2o")), out_is_output=True)
            ln_rows(vsum, B("vsum"), 1e-5)
            ln_apply(vsum, B("vsum"), yout, B("yout"))
            import os as _os
            if _os.environ.get("DELAYFIX", "0") == "1":
                kb.op("dve", lambda e: e.tensor_copy(out=sgcol[:, 0:8],
                                                     in_=yout[:, 0, 0:8]),
                      reads=(B("yout"),), writes=(B("sgcol"),), small=True)
                kb.op("dve", lambda e: e.tensor_copy(out=sgcol[:, 8:16],
                                                     in_=sgcol[:, 0:8]),
                      reads=(B("sgcol"),), writes=(B("sgcol"),), small=True)
                kb.dma("sp", lambda e, s_=s_, t0=t0: e.dma_start(
                    out=xs_chunk_ap(y_t, s_, t0, TCF), in_=yout[:]),
                    reads=(B("yout"), B("sgcol")), out_is_output=True)
            else:
                kb.dma("sp", lambda e, s_=s_, t0=t0: e.dma_start(
                    out=xs_chunk_ap(y_t, s_, t0, TCF), in_=yout[:]),
                    reads=(B("yout"),), out_is_output=True)
        if _DUMP:
            for nm, h, shp, dt_ in [
                ("xln", xln, [128, DC, TCF], F32), ("t1", t1, [128, OC_H, TCF], BF16),
                ("t2", t2, [128, OC_H, TCF], BF16), ("vsum", vsum, [128, DC, TCF], F32),
            ]:
                dt2 = nc.dram_tensor("dbg7_" + nm, shp, dt_, kind="ExternalOutput")
                kb.dma("sp", lambda e, d=dt2, src=h: e.dma_start(out=d.ap(), in_=src[:]),
                       reads=(), out_is_output=True)

    def phase2():
        AR.reset()
        wg_h = AR.alloc("wgb", [128, DC, DFFP], BF16)
        wd_h = AR.alloc("wdb", [128, FC, D], BF16)
        wu_sb = [AR.alloc(f"wusb{i}", [128, DC, 128], BF16) for i in range(2)]
        sg = AR.alloc("sg", [128, FC, TCF], F32)
        tb = AR.alloc("tb", [128, FC, TCF], BF16)
        xch = AR.alloc("xch2", [128, DC, TCF], F32)
        xn = AR.alloc("xn2", [128, DC, TCF], BF16)
        sq = AR.alloc("sq2", [128, DC, TCF], BF16)

        wu_src = dr["wu"].ap().rearrange("(kc p) (oc o) -> oc p kc o", p=128, o=128)
        for oc in range(FC):
            kb.dma("pool", lambda e, oc=oc: e.dma_start(
                out=wub.ap()[oc].rearrange("p (kc o) -> p kc o", o=128),
                in_=wu_src[oc]), writes=(B("wub"),), stream="pre2")
        kb.dma("pool", lambda e: e.dma_start(
            out=wg_h[:], in_=dr["wg"].ap().rearrange("(kc p) o -> p kc o", p=128)),
            writes=(B("wgb"),), stream="pre2")
        kb.dma("pool", lambda e: e.dma_start(
            out=wd_h[:], in_=dr["wd"].ap().rearrange("(kc p) o -> p kc o", p=128)),
            writes=(B("wdb"),), stream="pre2")

        for s_ in range(NSEQ):
            for c in range(T // TCF):
                t0 = c * TCF
                kb.dma("sp", lambda e, s_=s_, t0=t0: e.dma_start(
                    out=xch[:], in_=xs_chunk_ap(xs_a, s_, t0, TCF)),
                    reads=(B("xs_a"),), writes=(B("xch2"),))
                rms_norm_chunk(xch, B("xch2"), gffn, B("c_gffn"), xn, B("xn2"), sq, TCF)
                for oc in range(FC):
                    kb.dma("sp", lambda e, oc=oc, w=wu_sb[oc % 2]: e.dma_start(
                        out=w[:], in_=wub.ap()[oc].rearrange("p (kc o) -> p kc o", o=128)),
                        reads=(B("wub"),), writes=(B(f"wusb{oc % 2}"),))
                    pg = PSB[(2 * oc) % 3]
                    pgB = B(f"ps{(2 * oc) % 3}")
                    pu = PSB[(2 * oc + 1) % 3]
                    puB = B(f"ps{(2 * oc + 1) % 3}")
                    kb.mm_group([(pg[:, 0:TCF], wg_h[:, kc, oc * 128:(oc + 1) * 128],
                                  xn[:, kc, :]) for kc in range(DC)],
                                reads=(B("xn2"), B("wgb")), psb=pgB)
                    kb.op("act", lambda e, oc=oc, pg=pg: e.activation(
                        out=sg[:, oc, :], in_=pg[:, 0:TCF], func=AF.Silu),
                        reads=(pgB,), writes=(B("sg"),))
                    kb.mm_group([(pu[:, 0:TCF], wu_sb[oc % 2][:, kc, :],
                                  xn[:, kc, :]) for kc in range(DC)],
                                reads=(B("xn2"), B(f"wusb{oc % 2}")), psb=puB)
                    kb.op("dve", lambda e, oc=oc, pu=pu: e.tensor_tensor(
                        out=tb[:, oc, :], in0=pu[:, 0:TCF], in1=sg[:, oc, :],
                        op=ALU.mult),
                        reads=(puB, B("sg")), writes=(B("tb"),))
                for oc2 in range(DC):
                    pd = PSB[oc2 % 3]
                    pdB = B(f"ps{oc2 % 3}")
                    kb.mm_group([(pd[:, 0:TCF], wd_h[:, kc, oc2 * 128:(oc2 + 1) * 128],
                                  tb[:, kc, :]) for kc in range(FC)],
                                reads=(B("tb"), B("wdb")), psb=pdB)
                    kb.op("dve", lambda e, oc2=oc2, pd=pd: e.tensor_add(
                        out=xch[:, oc2, :], in0=xch[:, oc2, :], in1=pd[:, 0:TCF]),
                        reads=(pdB,), writes=(B("xch2"),))
                kb.dma("sp", lambda e, s_=s_, t0=t0: e.dma_start(
                    out=xs_chunk_ap(xs_b, s_, t0, TCF), in_=xch[:]),
                    reads=(B("xch2"),), writes=(B("xs_b"),))


    def dump(name, h, shape, dt):
        if not _DUMP:
            return
        dt_ = nc.dram_tensor("dbg_" + name, shape, dt, kind="ExternalOutput")
        kb.dma("sp", lambda e, d=dt_, src=h: e.dma_start(out=d.ap(), in_=src[:]),
               reads=(), out_is_output=True)

    if not _ONLY_M0 and not _ONLY_P7:
        phase1()
        kb.barrier()
    xs_cur = "xs_a"
    if _PHASES >= 2 and not _ONLY_M0 and not _ONLY_P7:
        phase2()
        kb.barrier()
        xs_cur = "xs_b"
    xs_map = {"xs_a": xs_a, "xs_b": xs_b}
    if _ONLY_M0:
        phase_mamba_pipe(1, xin_t, B("xin"))
        kb.barrier()
        for s_ in range(NSEQ):
            kb.dma("sp", lambda e, s_=s_: e.dma_start(
                out=y_t.ap()[s_], in_=xs_a.ap()[s_]),
                reads=(B("xs_a"),), out_is_output=True)
        kb.finalize()
        return _replay(nc, kb, ctx)
    nl = min(_PHASES - 2, 3) if _PHASES >= 3 else 0
    if nl > 0 and not _ONLY_P7:
        phase_mamba_pipe(nl, xs_map[xs_cur], B(xs_cur))
        kb.barrier()
        xs_cur = ["xs_a", "xs_b", "xs_a"][nl - 1]

    if _ONLY_P7:
        phase7(xin_t, B("xin"))
        kb.finalize()
        return _replay(nc, kb, ctx)
    if _PHASES >= 7:
        phase7(xs_map[xs_cur], B(xs_cur))

    if _PHASES < 7:
        xs_t = {"xs_a": xs_a, "xs_b": xs_b}[xs_cur]
        for s in range(NSEQ):
            kb.dma("sp", lambda e, s=s, xs_t=xs_t: e.dma_start(
                out=y_t.ap()[s], in_=xs_t.ap()[s]),
                reads=(B(xs_cur),), out_is_output=True)

    kb.finalize()
    return _replay(nc, kb, ctx)


def _replay(nc, kb, ctx):
    sems = {}
    for s in kb.sem_names:
        sems[s] = ctx.enter_context(nc.semaphore("sem_" + s))

    with nc.Block() as block:
        def runner(eng_name):
            def run(engobj):
                seen = {}
                for op in kb.ops[eng_name]:
                    for (s, v) in op.waits:
                        if seen.get(s, -1) >= v:
                            continue
                        seen[s] = v
                        engobj.wait_ge(sems[s], v)
                    if op.fn is None:
                        continue
                    inst = op.fn(engobj)
                    if op.inc is not None:
                        inst.then_inc(sems[op.inc[0]], op.inc[1])
            return run
        block.tensor(runner("pe"))
        block.vector(runner("dve"))
        block.scalar(runner("act"))
        block.gpsimd(runner("pool"))
        block.sync(runner("sp"))

    ctx.close()
    return nc


def kernel(**inp):
    W = _prep_inputs(inp)
    x = np.asarray(inp["L0_attn_x"])
    nc = build_program()
    in_maps = []
    for c in range(8):
        m = dict(W)
        m["xin"] = np.ascontiguousarray(x[2 * c:2 * c + 2].transpose(0, 2, 1))
        in_maps.append(m)
    res = run_bass_kernel_spmd(nc, in_maps, core_ids=list(range(8)))
    outs = [r["y"] for r in res.results]
    y = np.concatenate([o.transpose(0, 2, 1) for o in outs], axis=0)
    return np.ascontiguousarray(y.astype(np.float32))


# version 19 (final)
# speedup vs baseline: 1.2545x; 1.2545x over previous
import numpy as np
from contextlib import ExitStack

import concourse.bass as bass
import concourse.mybir as mybir
from concourse.bass_utils import run_bass_kernel_spmd

F32 = mybir.dt.float32
BF16 = mybir.dt.bfloat16
AF = mybir.ActivationFunctionType
ALU = mybir.AluOpType
AX = mybir.AxisListType

D = 1024
T = 2048
NSEQ = 2
DC = 8
NH, NKV, HD = 16, 4, 64
DFF = 2752
DFFP = 2816
FC = DFFP // 128
NST = 16
KCONV = 4
DXP = 1152
DH = 4096
OC_H = DH // 128
NEG = -1.0e30

TCA = 512
TCF = 512
TCM = 256

_PHASES = 7
_DUMP = False
_MSTAGE = 9
_M2 = 9
_ONLY_M0 = False
_ONLY_P7 = False
_STRICT_SAME_ENGINE = False
_ARDBG = False
_P7CH = None

ENG = ("pe", "act", "dve", "pool", "sp")


class Buf:
    __slots__ = ("name", "wr", "rd", "small", "psum")

    def __init__(self, name):
        self.name = name
        self.wr = None
        self.rd = []
        self.small = False
        self.psum = name.startswith("ps")


class Op:
    __slots__ = ("eng", "fn", "waits", "inc")

    def __init__(self, eng, fn, waits, inc):
        self.eng = eng
        self.fn = fn
        self.waits = waits
        self.inc = inc


class KB:

    def __init__(self):
        self.ops = {e: [] for e in ENG}
        self.sem_names = ["pe", "act", "dve", "pool"]
        self.dma_sems = set()
        self.group_sems = set()
        self.cnt = {s: 0 for s in self.sem_names}
        self.waited = {e: {} for e in ENG}
        self.pending = {e: [] for e in ENG}
        self.out_tokens = []

    def _sem(self, name):
        if name not in self.cnt:
            self.sem_names.append(name)
            self.cnt[name] = 0
        return name

    def _emit(self, eng, fn, deps, inc_sem, inc_amt, fences=()):
        waits = []
        allw = [(s, v, False) for (s, v) in list(self.pending[eng]) + list(deps)]
        allw += [(s, v, True) for (s, v) in fences]
        self.pending[eng] = []
        if inc_sem in self.group_sems:
            allw = [(s, v, f) for (s, v, f) in allw if s != inc_sem]
        for (s, v, force) in allw:
            if v is None or v <= 0:
                continue
            if s == eng and not force and not _STRICT_SAME_ENGINE:
                continue
            if s in self.dma_sems:
                v = self.cnt[s]
            if self.waited[eng].get(s, 0) >= v:
                continue
            self.waited[eng][s] = v
            waits.append((s, v))
        inc = None
        if inc_sem is not None:
            self.cnt[inc_sem] += inc_amt
            inc = (inc_sem, inc_amt)
        self.ops[eng].append(Op(eng, fn, waits, inc))
        return (inc_sem, self.cnt[inc_sem]) if inc_sem else None

    @staticmethod
    def _deps_of(reads, writes):
        deps = []
        for b in reads:
            if b.wr:
                deps.append(b.wr)
        for b in writes:
            if b.wr:
                deps.append(b.wr)
            deps.extend(b.rd)
        return deps

    def op(self, eng, fn, reads=(), writes=(), tag="", small=False):
        deps = self._deps_of(reads, writes)
        for b in reads:
            if b.psum:
                deps = list(deps) + list(b.rd)
        fences = [b.wr for b in reads
                  if b.wr is not None and b.small and b.wr[0] == eng]
        sem = eng if eng != "sp" else "spd"
        tok = self._emit(eng, fn, deps, sem, 1, fences=fences)
        for b in reads:
            b.rd.append(tok)
        for b in writes:
            b.wr = tok
            b.rd = []
            b.small = small
        return tok

    def dma(self, eng, fn, reads=(), writes=(), out_is_output=False,
            stream=None):
        deps = self._deps_of(reads, writes)
        for b in reads:
            if b.psum:
                deps = list(deps) + list(b.rd)
        key = stream if stream is not None else (writes[0].name if writes else "out")
        sem = self._sem(f"d_{eng[0]}_{key}")
        self.dma_sems.add(sem)
        if stream is not None:
            self.group_sems.add(sem)
        tok = self._emit(eng, fn, deps, sem, 16)
        for b in reads:
            b.rd.append(tok)
        for b in writes:
            b.wr = tok
            b.rd = []
        if out_is_output:
            self.out_tokens.append(tok)
        return tok

    def mm_group(self, mm_list, reads=(), psb=None, tag=""):
        n = len(mm_list)
        assert n >= 1
        deps = self._deps_of(reads, (psb,) if psb else ())
        tok = None
        for i, (o, l, r) in enumerate(mm_list):
            st, sp_ = (i == 0), (i == n - 1)
            fn = (lambda e, o=o, l=l, r=r, st=st, sp_=sp_:
                  e.matmul(o, l, r, start=st, stop=sp_))
            tok = self._emit("pe", fn, deps if i == 0 else (),
                             "pe" if sp_ else None, 1)
        for b in reads:
            b.rd.append(tok)
        if psb is not None:
            psb.wr = tok
            psb.rd = []
        return tok

    def mm1(self, out_ap, l_ap, r_ap, start, stop, reads=(), psb=None):
        deps = self._deps_of(reads, (psb,) if psb is not None else ())
        fn = (lambda e, o=out_ap, l=l_ap, r=r_ap, st=start, sp_=stop:
              e.matmul(o, l, r, start=st, stop=sp_))
        tok = self._emit("pe", fn, deps, "pe", 1)
        for b in reads:
            b.rd.append(tok)
        if psb is not None:
            psb.wr = tok
            psb.rd = []
        return tok

    def transpose(self, out_ap, in_ap, ident_ap, reads=(), psb=None):
        deps = self._deps_of(reads, (psb,) if psb is not None else ())
        fn = (lambda e, o=out_ap, i=in_ap, d=ident_ap: e.transpose(o, i, d))
        tok = self._emit("pe", fn, deps, "pe", 1)
        for b in reads:
            b.rd.append(tok)
        if psb is not None:
            psb.wr = tok
            psb.rd = []
        return tok

    def barrier(self):
        toks = [(s, self.cnt[s]) for s in self.sem_names if self.cnt[s] > 0]
        for e in ENG:
            self.pending[e].extend(toks)

    def finalize(self):
        for e in ENG:
            for op_ in self.ops[e]:
                op_.waits = [(sm, self.cnt[sm] if sm in self.group_sems else v)
                             for (sm, v) in op_.waits]
        waits = []
        for s in sorted(self.dma_sems):
            v = self.cnt[s]
            if v > 0 and self.waited["sp"].get(s, 0) < v:
                self.waited["sp"][s] = v
                waits.append((s, v))
        self.ops["sp"].append(Op("sp", None, waits, None))



def _prep_inputs(inp):
    import ml_dtypes
    W = {}
    W["wq"] = np.ascontiguousarray(inp["L0_attn_Wq"][0])
    W["wk"] = np.ascontiguousarray(inp["L0_attn_Wk"][0])
    W["wv"] = np.ascontiguousarray(inp["L0_attn_Wv"][0])
    W["wo"] = np.ascontiguousarray(inp["L0_attn_Wo"][0])
    W["wg"] = np.ascontiguousarray(
        np.pad(inp["L0_attn_Wg"][0], ((0, 0), (0, DFFP - DFF))))
    W["wu"] = np.ascontiguousarray(
        np.pad(inp["L0_attn_Wu"][0], ((0, 0), (0, DFFP - DFF))))
    W["wd"] = np.ascontiguousarray(
        np.pad(inp["L0_attn_Wd"][0], ((0, DFFP - DFF), (0, 0))))
    for l in range(3):
        W[f"win{l}"] = np.ascontiguousarray(inp["L0_attn_Win"][l])
        W[f"wx{l}"] = np.ascontiguousarray(
            np.pad(inp["L0_attn_Wx"][l], ((0, 0), (0, DXP - (D + 2 * NST)))))
        W[f"wout{l}"] = np.ascontiguousarray(inp["L0_attn_Wout"][l])
        W[f"cw{l}"] = np.ascontiguousarray(
            inp["L0_attn_conv_w"][l].reshape(DC, 128, KCONV).transpose(1, 0, 2))
        W[f"cb{l}"] = np.ascontiguousarray(
            inp["L0_attn_conv_b"][l].reshape(DC, 128).T)
        W[f"gm{l}"] = np.ascontiguousarray(
            inp["L0_attn_m_norm_w"][l].reshape(DC, 128).T)
        W[f"dp{l}"] = np.ascontiguousarray(
            inp["L0_attn_Dp"][l].reshape(DC, 128).T)
        W[f"alog{l}"] = np.ascontiguousarray(
            inp["L0_attn_A_log"][l].reshape(DC, 128, NST).transpose(1, 0, 2))
    W["gattn"] = np.ascontiguousarray(
        inp["L0_attn_attn_norm_w"][0].reshape(DC, 128).T)
    W["gffn"] = np.ascontiguousarray(
        inp["L0_attn_ffn_norm_w"][0].reshape(DC, 128).T)
    W["w1t"] = np.ascontiguousarray(inp["L0_ffn_w1_w"].T)
    W["w2t"] = np.ascontiguousarray(inp["L0_ffn_w2_w"].T)
    W["h1t"] = np.ascontiguousarray(inp["L0_ffn_h1_w"].T)
    W["h2t"] = np.ascontiguousarray(inp["L0_ffn_h2_w"].T)
    W["w3t"] = np.ascontiguousarray(inp["L0_ffn_w3_w"].T)
    W["s1"] = np.ascontiguousarray(inp["L0_ffn_w1_s"].reshape(OC_H, 128).T)
    W["s2"] = np.ascontiguousarray(inp["L0_ffn_w2_s"].reshape(OC_H, 128).T)
    W["sh1"] = np.ascontiguousarray(inp["L0_ffn_h1_s"].reshape(OC_H, 128).T)
    W["sh2"] = np.ascontiguousarray(inp["L0_ffn_h2_s"].reshape(OC_H, 128).T)
    W["s3"] = np.ascontiguousarray(inp["L0_ffn_w3_s"].reshape(DC, 128).T)
    tri = np.tril(np.ones((128, 128), dtype=bool))
    W["maskd"] = np.where(tri, 0.0, NEG).astype(np.float32)
    W["maskdt"] = np.ascontiguousarray(W["maskd"].T)
    import ml_dtypes
    W["ident"] = np.eye(128, dtype=ml_dtypes.bfloat16)
    return W



def build_program():
    nc = bass.Bass("TRN2", target_bir_lowering=False)
    kb = KB()

    xin_t = nc.dram_tensor("xin", [NSEQ, D, T], F32, kind="ExternalInput")
    y_t = nc.dram_tensor("y", [NSEQ, D, T], F32, kind="ExternalOutput")
    dr = {}
    for nm, shp, dt in [
        ("wq", [D, D], F32), ("wk", [D, 256], F32), ("wv", [D, 256], F32),
        ("wo", [D, D], F32),
        ("wg", [D, DFFP], F32), ("wu", [D, DFFP], F32), ("wd", [DFFP, D], F32),
        ("gattn", [128, DC], F32), ("gffn", [128, DC], F32),
        ("w1t", [D, DH], F32), ("w2t", [D, DH], F32),
        ("h1t", [DH, DH], F32), ("h2t", [DH, DH], F32), ("w3t", [DH, D], F32),
        ("s1", [128, OC_H], F32), ("s2", [128, OC_H], F32),
        ("sh1", [128, OC_H], F32), ("sh2", [128, OC_H], F32),
        ("s3", [128, DC], F32),
        ("maskd", [128, 128], F32), ("maskdt", [128, 128], F32),
        ("ident", [128, 128], BF16),
    ]:
        dr[nm] = nc.dram_tensor(nm, shp, dt, kind="ExternalInput")
    for l in range(3):
        dr[f"win{l}"] = nc.dram_tensor(f"win{l}", [D, 2 * D], F32, kind="ExternalInput")
        dr[f"wx{l}"] = nc.dram_tensor(f"wx{l}", [D, DXP], F32, kind="ExternalInput")
        dr[f"wout{l}"] = nc.dram_tensor(f"wout{l}", [D, D], F32, kind="ExternalInput")
        dr[f"cw{l}"] = nc.dram_tensor(f"cw{l}", [128, DC, KCONV], F32, kind="ExternalInput")
        dr[f"cb{l}"] = nc.dram_tensor(f"cb{l}", [128, DC], F32, kind="ExternalInput")
        dr[f"gm{l}"] = nc.dram_tensor(f"gm{l}", [128, DC], F32, kind="ExternalInput")
        dr[f"dp{l}"] = nc.dram_tensor(f"dp{l}", [128, DC], F32, kind="ExternalInput")
        dr[f"alog{l}"] = nc.dram_tensor(f"alog{l}", [128, DC, NST], F32, kind="ExternalInput")
    xs_a = nc.dram_tensor("xs_a", [NSEQ, D, T], F32)
    xs_b = nc.dram_tensor("xs_b", [NSEQ, D, T], F32)
    w1b = nc.dram_tensor("w1b", [OC_H, 128, DC * 128], BF16)
    w2b = nc.dram_tensor("w2b", [OC_H, 128, DC * 128], BF16)
    h1b = nc.dram_tensor("h1b", [OC_H, 128, OC_H * 128], BF16)
    h2b = nc.dram_tensor("h2b", [OC_H, 128, OC_H * 128], BF16)
    w3b = nc.dram_tensor("w3b", [DC, 128, OC_H * 128], BF16)
    wub = nc.dram_tensor("wub", [FC, 128, DC * 128], BF16)
    wxb = [nc.dram_tensor(f"wxb{l}", [DXP // 128, 128, DC * 128], BF16) for l in range(3)]
    winb = [nc.dram_tensor(f"winb{l}", [16, 128, DC * 128], BF16) for l in range(3)]
    woutb = [nc.dram_tensor(f"woutb{l}", [DC, 128, DC * 128], BF16) for l in range(3)]

    bufs = {}

    def B(name):
        if name not in bufs:
            bufs[name] = Buf(name)
        return bufs[name]

    ctx = ExitStack()
    sbt = {}

    def sb(name, shape, dt):
        assert "__frozen__" not in sbt, f"permanent alloc {name} after arena start"
        h = ctx.enter_context(nc.sbuf_tensor(name, shape, dt))
        sbt[name] = h
        return h

    ident = sb("sident", [128, 128], BF16)
    maskd = sb("smaskd", [128, 128], F32)
    maskdt = sb("smaskdt", [128, 128], F32)
    ones_r = sb("ones_r", [1, 128], BF16)
    ones_r32 = sb("ones_r32", [1, 128], F32)
    ones_c = sb("ones_c", [128, 1], BF16)
    gattn = sb("sgattn", [128, DC], F32)
    gffn = sb("sgffn", [128, DC], F32)
    gm = [sb(f"sgm{l}", [128, DC], F32) for l in range(3)]
    dp = [sb(f"sdp{l}", [128, DC], F32) for l in range(3)]
    cw = [sb(f"scw{l}", [128, DC, KCONV], F32) for l in range(3)]
    cb = [sb(f"scb{l}", [128, DC], F32) for l in range(3)]
    amat = [sb(f"samat{l}", [128, DC, NST], F32) for l in range(3)]
    s1v = sb("ss1", [128, OC_H], F32)
    s2v = sb("ss2", [128, OC_H], F32)
    sh1v = sb("ssh1", [128, OC_H], F32)
    sh2v = sb("ssh2", [128, OC_H], F32)
    s3v = sb("ss3", [128, DC], F32)
    carry = sb("carry", [128, DC, NST], F32)
    halo = sb("halo", [128, DC, KCONV - 1], F32)
    rowst = sb("rowst", [1, 3 * TCF], F32)
    
    cst = sb("cst", [128, 8 * NST], F32)
    cbfa = sb("cbf_a", [128, 16], BF16)
    cbfb = sb("cbf_b", [128, 16], BF16)
    ctr = sb("ctr", [16, 256], BF16)

    psS = ctx.enter_context(nc.psum_tensor("psS", [128, 2048], F32))
    ps0 = ctx.enter_context(nc.psum_tensor("ps0", [128, 512], F32))
    ps1 = ctx.enter_context(nc.psum_tensor("ps1", [128, 512], F32))
    ps2 = ctx.enter_context(nc.psum_tensor("ps2", [128, 512], F32))
    psT = ctx.enter_context(nc.psum_tensor("psT", [128, 1024], BF16))
    PSB = [ps0, ps1, ps2]

    PART_BYTES = nc.SBUF_PARTITION_SIZE_BYTES

    class Arena:
        def __init__(self):
            self.base = None
            self.cur = 0
            self.peak = 0
            self.n = 0

        def start(self):
            if self.base is None:
                self.base = (nc._sbuf_addr_for_side("left") + 31) // 32 * 32
                sbt["__frozen__"] = True

        def alloc(self, name, shape, dt):
            self.start()
            per = int(np.prod(shape[1:])) * mybir.dt.size(dt)
            per = (per + 31) // 32 * 32
            off = self.base + self.cur
            self.cur += per
            self.peak = max(self.peak, self.cur)
            assert off + per <= PART_BYTES, (
                f"sbuf arena overflow: {name} at {off}+{per} > {PART_BYTES}")
            self.n += 1
            h = nc.alloc_sbuf_tensor_at(f"ar_{name}_{self.n}", shape, dt, offset=off)
            return h

        def reset(self):
            if _ARDBG and self.base is not None:
                print(f"[arena] base={self.base} peak={self.peak} "
                      f"free={PART_BYTES - self.base - self.peak}")
            self.cur = 0
            self.peak = 0

    AR = Arena()

    sem_handles = {}

    def t_ap(h):
        return h[:]

    def xin_chunk_ap(s, t0, tc):
        return xin_t.ap()[s].rearrange("(dc p) t -> p dc t", p=128)[:, :, t0:t0 + tc]

    def xs_chunk_ap(xs, s, t0, tc):
        return xs.ap()[s].rearrange("(dc p) t -> p dc t", p=128)[:, :, t0:t0 + tc]

    def w_nat_ap(drt, kcn):
        return drt.ap().rearrange("(kc p) o -> p kc o", p=128)

    for nm, dst in [("ident", ident), ("maskd", maskd), ("maskdt", maskdt),
                    ("gattn", gattn), ("gffn", gffn),
                    ("s1", s1v), ("s2", s2v), ("sh1", sh1v), ("sh2", sh2v),
                    ("s3", s3v)]:
        kb.dma("sp", lambda e, d=dst, s=dr[nm]: e.dma_start(out=d[:], in_=s.ap()),
               writes=(B("c_" + nm),), stream="const")
    for l in range(3):
        for nm, dst in [(f"gm{l}", gm[l]), (f"dp{l}", dp[l]), (f"cw{l}", cw[l]),
                        (f"cb{l}", cb[l])]:
            kb.dma("sp", lambda e, d=dst, s=dr[nm]: e.dma_start(out=d[:], in_=s.ap()),
                   writes=(B("c_" + nm),), stream="const")
        kb.dma("sp", lambda e, d=amat[l], s=dr[f"alog{l}"]:
               e.dma_start(out=d[:], in_=s.ap()), writes=(B(f"amat{l}"),),
               stream="const")
        kb.op("act", lambda e, d=amat[l]: e.activation(out=d[:], in_=d[:], func=AF.Exp),
              reads=(), writes=(B(f"amat{l}"),), small=True)
        kb.op("dve", lambda e, d=amat[l]: e.tensor_scalar_mul(out=d[:], in0=d[:], scalar1=-1.0),
              writes=(B(f"amat{l}"),), small=True)
    kb.op("dve", lambda e: e.memset(ones_r[:], 1.0), writes=(B("ones"),), small=True)
    kb.op("dve", lambda e: e.memset(ones_r32[:], 1.0), writes=(B("ones"),), small=True)
    kb.op("dve", lambda e: e.memset(ones_c[:], 1.0), writes=(B("ones"),), small=True)

    _cv_cache = {}

    def cv(val):
        val = float(val)
        assert val in _cv_cache, f"cv({val}) not pre-registered"
        return _cv_cache[val][:, 0:1]

    def cvB():
        return B("consts")

    for _v in (1.0 / D, 1e-6, 1e-5, -0.5, float(HD) ** -0.5, 1.0, -1.0,
               1.0 / DH, 0.5):
        _v = float(_v)
        if _v not in _cv_cache:
            h = sb(f"cv{len(_cv_cache)}", [128, 1], F32)
            kb.op("dve", lambda e, h=h, v=_v: e.memset(h[:], v), writes=(B("consts"),), small=True)
            _cv_cache[_v] = h


    def rms_norm_chunk(xbuf_h, xB, g_h, gB, xn_h, xnB, sq_h, tc, eps=1e-6,
                       sqB=None):
        if sqB is None:
            sqB = B("sq")
        kb.op("act", lambda e: e.activation(out=sq_h[:], in_=xbuf_h[:], func=AF.Square),
              reads=(xB,), writes=(sqB,))
        mm = [(ps0[0:1, 0:tc], ones_c[:], sq_h[:, kc, :]) for kc in range(DC)]
        kb.mm_group(mm, reads=(sqB, B("ones")), psb=B("ps0"))
        kb.op("act", lambda e, sc=cv(1.0 / D)[0:1, :], bi=cv(eps)[0:1, :]:
              e.activation(out=rowst[0:1, 0:tc], in_=ps0[0:1, 0:tc],
                           func=AF.Ln, scale=sc, bias=bi),
              reads=(B("ps0"), cvB()), writes=(B("rowst"),), small=True)
        kb.op("act", lambda e, sc=cv(-0.5)[0:1, :], tc=tc: e.activation(
            out=rowst[0:1, tc:2 * tc], in_=rowst[0:1, 0:tc], func=AF.Exp, scale=sc),
              reads=(B("rowst"), cvB()), writes=(B("rowst"),), small=True)
        kb.mm_group([(ps1[:, 0:tc], ones_r32[:], rowst[0:1, tc:2 * tc])],
                    reads=(B("rowst"), B("ones")), psb=B("ps1"))
        assert tc <= 2 * TCF
        for dc in range(DC):
            kb.op("dve", lambda e, dc=dc: e.scalar_tensor_tensor(
                out=xn_h[:, dc, :], in0=xbuf_h[:, dc, :], scalar=g_h[:, dc:dc + 1],
                in1=ps1[:, 0:tc], op0=ALU.mult, op1=ALU.mult),
                reads=(xB, B("ps1"), gB), writes=(xnB,))

    def proj_mm(xn_h, xnB, w_h, wB, oc, tc, psb_t, psb_B, kcs=DC):
        mm = [(psb_t[:, 0:tc], w_h[:, kc, oc * 128:(oc + 1) * 128], xn_h[:, kc, 0:tc])
              for kc in range(kcs)]
        kb.mm_group(mm, reads=(xnB, wB), psb=psb_B)

    def precast_outer():
        def pc(src_dr, dst_dr, noc, nkc):
            srcv = src_dr.ap().rearrange("(kc p) (oc o) -> oc p kc o", p=128, o=128)
            nm = dst_dr.name
            for oc in range(noc):
                if nkc <= 16:
                    kb.dma("pool", lambda e, oc=oc, dst_dr=dst_dr, srcv=srcv:
                           e.dma_start(
                        out=dst_dr.ap()[oc].rearrange("p (kc o) -> p kc o", o=128),
                        in_=srcv[oc]), writes=(B(nm),), stream="pre7")
                else:
                    for hh in range(2):
                        k0, k1 = hh * nkc // 2, (hh + 1) * nkc // 2
                        kb.dma("pool", lambda e, oc=oc, k0=k0, k1=k1,
                               dst_dr=dst_dr, srcv=srcv: e.dma_start(
                            out=dst_dr.ap()[oc].rearrange(
                                "p (kc o) -> p kc o", o=128)[:, k0:k1, :],
                            in_=srcv[oc][:, k0:k1, :]), writes=(B(nm),),
                            stream="pre7")
        pc(dr["w1t"], w1b, OC_H, DC)
        pc(dr["w2t"], w2b, OC_H, DC)
        pc(dr["h1t"], h1b, OC_H, OC_H)
        pc(dr["h2t"], h2b, OC_H, OC_H)
        pc(dr["w3t"], w3b, DC, OC_H)

    def phase1():
        AR.reset()
        wq_h = AR.alloc("wqb", [128, DC, D], BF16)
        wk_h = AR.alloc("wkb", [128, DC, 256], BF16)
        wv_h = AR.alloc("wvb", [128, DC, 256], BF16)
        wo_h = AR.alloc("wob", [128, DC, D], BF16)
        Q = AR.alloc("Q", [128, DC, T], BF16)
        Kaug = [AR.alloc(f"Kaug{g}", [66, T], BF16) for g in range(NKV)]
        VT = AR.alloc("VT", [128, 16, 4 * 65], BF16)
        Qaug = [AR.alloc(f"Qaug{i}", [66, T], BF16) for i in range(2)]
        Ofm = AR.alloc("Ofm", [128, DC, T], BF16)
        xch = AR.alloc("xch", [128, DC, TCA], F32)
        xn = AR.alloc("xn", [128, DC, TCA], BF16)
        sq = AR.alloc("sqa", [128, DC, TCA], BF16)
        vtmp = AR.alloc("vtmp", [128, 2, TCA], BF16)
        pt = [AR.alloc(f"pt{i}", [128, 512], BF16) for i in range(2)]
        lrow = AR.alloc("lrow", [1, T], F32)
        rbc = AR.alloc("rbc", [64, T], BF16)
        csts = [AR.alloc(f"csts{i}", [128, 3 * NST], F32) for i in range(2)]
        cbf = [AR.alloc(f"cbf{i}", [128, 32], BF16) for i in range(2)]
        ctr2 = [AR.alloc(f"ctr2{i}", [16, 256], BF16) for i in range(2)]

        for nm, dst in [("wq", wq_h), ("wk", wk_h), ("wv", wv_h), ("wo", wo_h)]:
            kb.dma("pool", lambda e, d=dst, s=dr[nm]:
                   e.dma_start(out=d[:], in_=s.ap().rearrange("(kc p) o -> p kc o", p=128)),
                   writes=(B(nm + "b"),), stream="pre1")
        if _PHASES >= 7:
            precast_outer()

        kb.op("dve", lambda e: e.memset(VT[:, :, 64::65], 1.0),
              writes=(B("VT"),), small=True)

        def qkv(s_):
            for g in range(NKV):
                kb.op("dve", lambda e, g=g: e.memset(Kaug[g][64:66, :], 1.0),
                      writes=(B(f"kaug{g}"),), small=True)
            for c in range(T // TCA):
                t0 = c * TCA
                kb.dma("sp", lambda e, s_=s_, t0=t0: e.dma_start(
                    out=xch[:], in_=xin_chunk_ap(s_, t0, TCA)), writes=(B("xch"),))
                rms_norm_chunk(xch, B("xch"), gattn, B("c_gattn"), xn, B("xn"),
                               sq, TCA)
                for oc in range(DC):
                    pb = PSB[oc % 2]
                    proj_mm(xn, B("xn"), wq_h, B("wqb"), oc, TCA, pb, B(f"ps{oc % 2}"))
                    kb.op("act", lambda e, oc=oc, t0=t0, pb=pb,
                          sc=cv(float(HD) ** -0.5): e.activation(
                        out=Q[:, oc, t0:t0 + TCA], in_=pb[:, 0:TCA], func=AF.Copy,
                        scale=sc),
                        reads=(B(f"ps{oc % 2}"), cvB()), writes=(B("Q"),))
                for oc in range(2):
                    proj_mm(xn, B("xn"), wk_h, B("wkb"), oc, TCA, ps2, B("ps2"))
                    for half in range(2):
                        g = oc * 2 + half
                        kb.op("act", lambda e, g=g, half=half, t0=t0: e.activation(
                            out=Kaug[g][0:64, t0:t0 + TCA],
                            in_=ps2[half * 64:(half + 1) * 64, 0:TCA], func=AF.Copy),
                            reads=(B("ps2"),), writes=(B(f"kaug{g}"),))
                for oc in range(2):
                    proj_mm(xn, B("xn"), wv_h, B("wvb"), oc, TCA, ps2, B("ps2"))
                    kb.op("dve", lambda e, oc=oc: e.tensor_copy(
                        out=vtmp[:, oc, :], in_=ps2[:, 0:TCA]),
                        reads=(B("ps2"),), writes=(B("vtmp"),))
                for kc4 in range(TCA // 128):
                    for oc in range(2):
                        j = kc4 * 2 + oc
                        kb.transpose(psT[:, j * 128:(j + 1) * 128],
                                     vtmp[:, oc, kc4 * 128:(kc4 + 1) * 128],
                                     ident[:], reads=(B("vtmp"), B("c_ident")),
                                     psb=B("psT"))
                for kc4 in range(TCA // 128):
                    for oc in range(2):
                        j = kc4 * 2 + oc
                        kc = c * 4 + kc4
                        dst = VT[:, kc, oc * 130:(oc + 1) * 130].rearrange(
                            "p (g v) -> p g v", g=2)[:, :, 0:64]
                        kb.op("act", lambda e, j=j, dst=dst: e.activation(
                            out=dst, in_=psT[:, j * 128:(j + 1) * 128].rearrange(
                                "p (g v) -> p g v", g=2), func=AF.Copy),
                            reads=(B("psT"),), writes=(B("VT"),))

        def pass1_steps(s_, h):
            steps = []
            emit = steps.append
            return _pass1_build(s_, h, emit) or steps

        def _pass1_build(s_, h, emit):
            hp = (h % 2) * 64
            hc = h // 2
            g = h // NKV
            hh = h % 2
            qa, qaB = Qaug[hh], B(f"qaug{hh}")
            cst_, cstB = csts[hh], B(f"csts{hh}")
            def st_qcopy():
                kb.op("dve", lambda e, hp=hp, hc=hc, qa=qa: e.tensor_copy(
                    out=qa[0:64, :], in_=Q[hp:hp + 64, hc, :]),
                    reads=(B("Q"),), writes=(qaB,))
            emit(st_qcopy)
            def st_qk(qt, ci, kb0, kw):
                kb.mm_group([(ps2[:, 0:kw], qa[0:64, qt * 128:qt * 128 + 128],
                              Kaug[g][0:64, kb0:kb0 + kw])],
                            reads=(qaB, B(f"kaug{g}")), psb=B("ps2"))
                if ci == 0:
                    kb.op("dve", lambda e, qt=qt, kw=kw, cst_=cst_:
                          e.tensor_reduce(
                        out=cst_[:, qt:qt + 1], in_=ps2[:, 0:kw], axis=AX.X,
                        op=ALU.max),
                        reads=(B("ps2"),), writes=(cstB,), small=True)
                else:
                    kb.op("dve", lambda e, qt=qt, kw=kw, cst_=cst_:
                          e.tensor_reduce(
                        out=cst_[:, 16 + qt:17 + qt], in_=ps2[:, 0:kw],
                        axis=AX.X, op=ALU.max),
                        reads=(B("ps2"),), writes=(cstB,), small=True)
                    kb.op("dve", lambda e, qt=qt, cst_=cst_: e.tensor_max(
                        out=cst_[:, qt:qt + 1], in0=cst_[:, qt:qt + 1],
                        in1=cst_[:, 16 + qt:17 + qt]),
                        reads=(cstB,), writes=(cstB,), small=True)
            for qt in range(16):
                kmax = (qt + 1) * 128
                for ci, kb0 in enumerate(range(0, kmax, 512)):
                    kw = min(512, kmax - kb0)
                    emit(lambda qt=qt, ci=ci, kb0=kb0, kw=kw: st_qk(qt, ci, kb0, kw))

            def st_tail():
                _pass1_tail(h, cst_, cstB, qa, qaB)
            emit(st_tail)

        def _pass1_tail(h, cst_, cstB, qa, qaB):
            hh = h % 2
            kb.op("dve", lambda e, cst_=cst_: e.tensor_scalar_mul(
                out=cst_[:, 32:48], in0=cst_[:, 0:16], scalar1=-1.0),
                reads=(cstB,), writes=(cstB,), small=True)
            cb_ = cbf[hh]
            cbB = B(f"cbf{hh}")
            kb.op("dve", lambda e, cst_=cst_, cb_=cb_: e.tensor_copy(
                out=cb_[:, 0:16], in_=cst_[:, 32:48]),
                reads=(cstB,), writes=(cbB,), small=True)
            kb.op("dve", lambda e, cst_=cst_, cb_=cb_: e.tensor_sub(
                out=cst_[:, 32:48], in0=cst_[:, 32:48], in1=cb_[:, 0:16]),
                reads=(cstB, cbB), writes=(cstB,), small=True)
            kb.op("dve", lambda e, cst_=cst_, cb_=cb_: e.tensor_copy(
                out=cb_[:, 16:32], in_=cst_[:, 32:48]),
                reads=(cstB,), writes=(cbB,), small=True)
            ct_ = ctr2[hh]
            ctB = B(f"ctr2{hh}")
            kb.transpose(psT[0:16, 0:128], cb_[:, 0:16], ident[:],
                         reads=(cbB, B("c_ident")), psb=B("psT"))
            kb.op("act", lambda e, ct_=ct_: e.activation(
                out=ct_[0:16, 0:128], in_=psT[0:16, 0:128], func=AF.Copy),
                reads=(B("psT"),), writes=(ctB,))
            kb.transpose(psT[0:16, 128:256], cb_[:, 16:32], ident[:],
                         reads=(cbB, B("c_ident")), psb=B("psT"))
            kb.op("act", lambda e, ct_=ct_: e.activation(
                out=ct_[0:16, 128:256], in_=psT[0:16, 128:256], func=AF.Copy),
                reads=(B("psT"),), writes=(ctB,))
            kb.dma("pool", lambda e, qa=qa, ct_=ct_: e.dma_start(
                out=qa[64:65, :].rearrange("o (n c) -> o n c", n=16),
                in_=ct_[0:16, 0:128]),
                reads=(ctB,), writes=(qaB,))
            kb.dma("pool", lambda e, qa=qa, ct_=ct_: e.dma_start(
                out=qa[65:66, :].rearrange("o (n c) -> o n c", n=16),
                in_=ct_[0:16, 128:256]),
                reads=(ctB,), writes=(qaB,))

        def pass2_steps(s_, h):
            steps = []
            emit = steps.append
            hp = (h % 2) * 64
            hc = h // 2
            g = h // NKV
            hh = h % 2
            qa, qaB = Qaug[hh], B(f"qaug{hh}")
            def st_A(kc, qb, pi):
                qb0 = (kc * 128) // 512
                qlo = max(qb * 512, kc * 128)
                qhi = (qb + 1) * 512
                wdt = qhi - qlo
                stp = PSB[pi]
                sB = B(f"ps{pi}")
                ptt = pt[pi]
                ptB = B(f"pt{pi}")
                kb.mm_group([(stp[:, 0:wdt], Kaug[g][:, kc * 128:(kc + 1) * 128],
                              qa[:, qlo:qhi])],
                            reads=(B(f"kaug{g}"), qaB), psb=sB)
                if qb == qb0:
                    kb.op("dve", lambda e, stp=stp: e.tensor_add(
                        out=stp[:, 0:128], in0=stp[:, 0:128], in1=maskdt[:]),
                        reads=(B("c_maskdt"),), writes=(sB,))
                kb.op("act", lambda e, stp=stp, wdt=wdt, ptt=ptt: e.activation(
                    out=ptt[:, 0:wdt], in_=stp[:, 0:wdt], func=AF.Exp),
                    reads=(sB,), writes=(ptB,))

            def st_B(kc, qb, pi):
                qlo = max(qb * 512, kc * 128)
                qhi = (qb + 1) * 512
                wdt = qhi - qlo
                ptt = pt[pi]
                ptB = B(f"pt{pi}")
                kb.mm1(psS[0:65, qlo:qhi],
                       VT[:, kc, g * 65:(g + 1) * 65],
                       ptt[:, 0:wdt],
                       start=(kc == 0), stop=(kc == 4 * qb + 3),
                       reads=(B("VT"), ptB), psb=B("psS"))

            tiles = [(kc, qb) for kc in range(16)
                     for qb in range((kc * 128) // 512, 4)]
            emit(lambda: st_A(*tiles[0], 0))
            for i in range(len(tiles)):
                if i + 1 < len(tiles):
                    emit(lambda i=i: st_A(*tiles[i + 1], (i + 1) % 2))
                emit(lambda i=i: st_B(*tiles[i], i % 2))
            emit(lambda: _pass2_tail(hp, hc))
            return steps

        def _pass2_tail(hp, hc):
            kb.op("dve", lambda e: e.tensor_copy(out=lrow[0:1, :],
                                                 in_=psS[64:65, 0:T]),
                  reads=(B("psS"),), writes=(B("lrow"),), small=True)
            kb.op("dve", lambda e: e.reciprocal(out=lrow[0:1, :], in_=lrow[0:1, :]),
                  reads=(B("lrow"),), writes=(B("lrow"),), small=True)
            for blk in range(4):
                kb.mm_group([(psS[64:128, blk * 512:(blk + 1) * 512],
                              ones_r32[0:1, 0:64],
                              lrow[0:1, blk * 512:(blk + 1) * 512])],
                            reads=(B("lrow"), B("ones")), psb=B("psS"))
                kb.op("act", lambda e, blk=blk: e.activation(
                    out=rbc[:, blk * 512:(blk + 1) * 512],
                    in_=psS[64:128, blk * 512:(blk + 1) * 512],
                    func=AF.Copy), reads=(B("psS"),), writes=(B("rbc"),))
            for blk in range(4):
                kb.op("dve", lambda e, hp=hp, hc=hc, blk=blk: e.tensor_tensor(
                    out=Ofm[hp:hp + 64, hc, blk * 512:(blk + 1) * 512],
                    in0=psS[0:64, blk * 512:(blk + 1) * 512],
                    in1=rbc[:, blk * 512:(blk + 1) * 512], op=ALU.mult),
                    reads=(B("psS"), B("rbc")), writes=(B("Ofm"),))

        def wo_resid(s_):
            for c in range(T // TCA):
                t0 = c * TCA
                kb.dma("sp", lambda e, s_=s_, t0=t0: e.dma_start(
                    out=xch[:], in_=xin_chunk_ap(s_, t0, TCA)), writes=(B("xch"),))
                for oc in range(DC):
                    pb = PSB[oc % 2]
                    mm = [(pb[:, 0:TCA], wo_h[:, kc, oc * 128:(oc + 1) * 128],
                           Ofm[:, kc, t0:t0 + TCA]) for kc in range(DC)]
                    kb.mm_group(mm, reads=(B("Ofm"), B("wob")), psb=B(f"ps{oc % 2}"))
                    kb.op("dve", lambda e, oc=oc, pb=pb: e.tensor_add(
                        out=xch[:, oc, :], in0=xch[:, oc, :], in1=pb[:, 0:TCA]),
                        reads=(B(f"ps{oc % 2}"),), writes=(B("xch"),))
                kb.dma("sp", lambda e, s_=s_, t0=t0: e.dma_start(
                    out=xs_chunk_ap(xs_a, s_, t0, TCA), in_=xch[:]),
                    reads=(B("xch"),), writes=(B("xs_a"),))

        for s_ in range(NSEQ):
            qkv(s_)
            for st in pass1_steps(s_, 0):
                st()
            for h in range(NH):
                s1 = pass1_steps(s_, h + 1) if h + 1 < NH else []
                s2 = pass2_steps(s_, h)
                i = j = 0
                n1, n2 = len(s1), len(s2)
                while i < n2 or j < n1:
                    if i < n2:
                        s2[i]()
                        i += 1
                    while j < n1 and j * n2 <= i * n1:
                        s1[j]()
                        j += 1
            wo_resid(s_)


    def phase_mamba_pipe(nl, src0_t, src0B):
        AR.reset()
        TCE = TCM + 1
        NB = NST * TCM
        NU = 16 * nl
        NCH = T // TCM
        pSv = [psS[:, 512 * b:512 * (b + 1)] for b in range(4)]
        pSB = [B(f"psSb{b}") for b in range(4)]

        win_sb = [AR.alloc(f"mwin{i}", [128, DC, 128], BF16) for i in range(2)]
        wx_sb = [AR.alloc(f"mwx{i}", [128, DC, 128], BF16) for i in range(2)]
        wo_sb = [AR.alloc(f"mwo{i}", [128, DC, 128], BF16) for i in range(2)]
        xch = [AR.alloc(f"mxch{i}", [128, DC, TCM], F32) for i in range(3)]
        sz = [AR.alloc(f"msz{i}", [128, DC, TCM], BF16) for i in range(3)]
        xn = [AR.alloc(f"mxn{i}", [128, DC, TCM], BF16) for i in range(2)]
        xbh = [AR.alloc(f"mxbh{i}", [128, DC, KCONV - 1 + TCM], BF16)
               for i in range(2)]
        s2 = [AR.alloc(f"ms2{i}", [128, DC, TCM], BF16) for i in range(2)]
        xcb = [AR.alloc(f"mxcb{i}", [128, DC, TCM], BF16) for i in range(2)]
        delta = [AR.alloc(f"mdelta{i}", [128, DC, TCM], BF16) for i in range(2)]
        dxcb = [AR.alloc(f"mdxcb{i}", [128, DC, TCM], BF16) for i in range(2)]
        Bb = [AR.alloc(f"mBb{i}", [128, NST, TCM], BF16) for i in range(2)]
        Cb = [AR.alloc(f"mCb{i}", [128, NST, TCM], BF16) for i in range(2)]
        bcr = [AR.alloc(f"mbcr{i}", [32, TCM], BF16) for i in range(2)]
        abar = [AR.alloc(f"mabar{i}", [128, NST, TCE], BF16) for i in range(3)]
        bbym = [AR.alloc(f"mbby{i}", [128, NST, TCE], BF16) for i in range(3)]
        carry2 = [AR.alloc(f"mcarry{i}", [128, DC, NST], F32) for i in range(2)]
        ybuf = AR.alloc("mybuf", [128, DC, TCM], BF16)
        gb = [AR.alloc(f"mgb{i}", [128, DC, TCM], BF16) for i in range(2)]
        for i in range(3):
            kb.op("dve", lambda e, i=i: e.memset(abar[i][:, :, 0:1], 0.0),
                  writes=(B(f"mabar{i}"),), small=True)

        srcs = [(src0_t, src0B), (xs_a, B("xs_a")), (xs_b, B("xs_b"))]
        dsts = [(xs_a, B("xs_a")), (xs_b, B("xs_b")), (xs_a, B("xs_a"))]

        for l in range(nl):
            for src_nm, dst_t, noc in ((f"win{l}", winb[l], 16),
                                       (f"wx{l}", wxb[l], 9),
                                       (f"wout{l}", woutb[l], DC)):
                srcv = dr[src_nm].ap().rearrange("(kc p) (oc o) -> oc p kc o",
                                                 p=128, o=128)
                for oc in range(noc):
                    kb.dma("pool", lambda e, oc=oc, dst_t=dst_t, srcv=srcv:
                           e.dma_start(
                        out=dst_t.ap()[oc].rearrange("p (kc o) -> p kc o", o=128),
                        in_=srcv[oc]), writes=(B(dst_t.name),), stream=f"prem{l}")

        def stream_w(cache_t, oc, slot, slotB):
            kb.dma("sp", lambda e, oc=oc, w=slot, cache_t=cache_t: e.dma_start(
                out=w[:], in_=cache_t.ap()[oc].rearrange("p (kc o) -> p kc o", o=128)),
                reads=(B(cache_t.name),), writes=(slotB,))

        def U(u):
            return u // 16, (u % 16) % NSEQ, (u % 16) // NSEQ

        def st_bcast(u):
            p2 = u % 2
            for r0, r1, dstt, dstB, rowB, ei in (
                    (0, 16, Bb[p2], B(f"mBb{p2}"), B(f"mBbr{p2}"), "dve"),
                    (16, 32, Cb[p2], B(f"mCb{p2}"), B(f"mCbr{p2}"), "act")):
                kb.dma("pool", lambda e, rows=bcr[p2][r0:r1, :], dstt=dstt:
                       e.dma_start(out=dstt[0:1, :, :], in_=rows),
                       reads=(B(f"mbcr{p2}"),), writes=(rowB, dstB))
                flat = dstt[:].rearrange("p n t -> p (n t)")
                for j in range(NB // 512):
                    pb, pbB = PSB[j % 2], B(f"ps{j % 2}")
                    kb.mm_group([(pb[:, 0:512], ones_r[:],
                                  flat[0:1, j * 512:(j + 1) * 512])],
                                reads=(rowB, B("ones")), psb=pbB)
                    if ei == "dve":
                        kb.op("dve", lambda e, j=j, pb=pb, flat=flat: e.tensor_copy(
                            out=flat[:, j * 512:(j + 1) * 512], in_=pb[:, 0:512]),
                            reads=(pbB,), writes=(dstB,))
                    else:
                        kb.op("act", lambda e, j=j, pb=pb, flat=flat: e.activation(
                            out=flat[:, j * 512:(j + 1) * 512], in_=pb[:, 0:512],
                            func=AF.Copy),
                            reads=(pbB,), writes=(dstB,))

        def st_xch_dma(u):
            l, s_, c = U(u)
            src_t, srcB = srcs[l]
            kb.dma("sp", lambda e, xchp=xch[u % 3], src_t=src_t, s_=s_, c=c:
                   e.dma_start(out=xchp[:],
                               in_=xs_chunk_ap(src_t, s_, c * TCM, TCM)),
                   reads=(srcB,), writes=(B(f"mxch{u % 3}"),))

        def st_conv(u):
            l, s_, c = U(u)
            p2 = u % 2
            convt, convtB = xn[p2], B(f"mxn{p2}")
            xbhp, xbhB = xbh[s_], B(f"mxbh{s_}")
            s2p, s2B = s2[s_], B(f"ms2{s_}")
            kb.op("dve", lambda e, s2p=s2p, xbhp=xbhp: e.tensor_tensor(
                out=s2p[:], in0=xbhp[:, :, 0:TCM],
                in1=cw[l][:, :, 0:1].broadcast_to((128, DC, TCM)),
                op=ALU.mult),
                reads=(xbhB, B(f"c_cw{l}")), writes=(s2B,))
            for k in range(1, KCONV):
                kb.op("dve", lambda e, k=k, xbhp=xbhp: e.tensor_tensor(
                    out=convt[:], in0=xbhp[:, :, k:k + TCM],
                    in1=cw[l][:, :, k:k + 1].broadcast_to((128, DC, TCM)),
                    op=ALU.mult),
                    reads=(xbhB, B(f"c_cw{l}")), writes=(convtB,))
                kb.op("dve", lambda e, s2p=s2p: e.tensor_add(
                    out=s2p[:], in0=s2p[:], in1=convt[:]),
                    reads=(convtB, s2B), writes=(s2B,))
            kb.op("dve", lambda e, s2p=s2p: e.tensor_tensor(
                out=s2p[:], in0=s2p[:],
                in1=cb[l][:].unsqueeze(2).broadcast_to((128, DC, TCM)),
                op=ALU.add),
                reads=(B(f"c_cb{l}"), s2B), writes=(s2B,))

        def st_rms_actpe(u):
            l = u // 16
            p2 = u % 2
            xchp, xchB = xch[u % 3], B(f"mxch{u % 3}")
            xnp, xnB = xn[p2], B(f"mxn{p2}")
            kb.op("act", lambda e, xnp=xnp, xchp=xchp: e.activation(
                out=xnp[:], in_=xchp[:], func=AF.Square),
                reads=(xchB,), writes=(xnB,))
            kb.mm_group([(pSv[0][0:1, 0:TCM], ones_c[:], xnp[:, kc, :])
                         for kc in range(DC)],
                        reads=(xnB, B("ones")), psb=pSB[0])
            kb.op("act", lambda e, sc=cv(1.0 / D)[0:1, :], bi=cv(1e-6)[0:1, :]:
                  e.activation(out=rowst[0:1, 0:TCM], in_=pSv[0][0:1, 0:TCM],
                               func=AF.Ln, scale=sc, bias=bi),
                  reads=(pSB[0], cvB()), writes=(B("rowst"),), small=True)
            kb.op("act", lambda e, sc=cv(-0.5)[0:1, :]: e.activation(
                out=rowst[0:1, TCM:2 * TCM], in_=rowst[0:1, 0:TCM],
                func=AF.Exp, scale=sc),
                reads=(B("rowst"), cvB()), writes=(B("rowst"),), small=True)
            kb.mm_group([(pSv[0][:, 0:TCM], ones_r32[:],
                          rowst[0:1, TCM:2 * TCM])],
                        reads=(B("rowst"), B("ones")), psb=pSB[0])

        def st_rms_stt(u):
            l, s_, c = U(u)
            p2 = u % 2
            xchp, xchB = xch[u % 3], B(f"mxch{u % 3}")
            xnp, xnB = xn[p2], B(f"mxn{p2}")
            for dc in range(DC):
                kb.op("dve", lambda e, dc=dc, xnp=xnp, xchp=xchp:
                      e.scalar_tensor_tensor(
                    out=xnp[:, dc, :], in0=xchp[:, dc, :],
                    scalar=gm[l][:, dc:dc + 1], in1=pSv[0][:, 0:TCM],
                    op0=ALU.mult, op1=ALU.mult),
                    reads=(xchB, pSB[0], B(f"c_gm{l}")), writes=(xnB,))
            xbhp, xbhB = xbh[s_], B(f"mxbh{s_}")
            if c == 0:
                kb.op("dve", lambda e, xbhp=xbhp: e.memset(
                    xbhp[:, :, 0:KCONV - 1], 0.0),
                    writes=(xbhB,), small=True)
            else:
                kb.op("dve", lambda e, xbhp=xbhp: e.tensor_copy(
                    out=xbhp[:, :, 0:KCONV - 1],
                    in_=xbhp[:, :, TCM:TCM + KCONV - 1]),
                    reads=(xbhB,), writes=(xbhB,), small=True)

        def st_scan(u):
            l, s_, c = U(u)
            p2 = u % 2
            dlB, dxB = B(f"mdelta{p2}"), B(f"mdxcb{p2}")
            carB = B(f"mcarry{s_}")

            def bby_in(dc):
                pi = dc % 3
                bby, bbyB = bbym[pi], B(f"mbby{pi}")
                kb.op("pool", lambda e, dc=dc, bby=bby: e.tensor_tensor(
                    out=bby[:, :, 1:], in0=Bb[p2][:],
                    in1=dxcb[p2][:, dc, :].unsqueeze(1).broadcast_to(
                        (128, NST, TCM)), op=ALU.mult),
                    reads=(B(f"mBb{p2}"), dxB), writes=(bbyB,))

            def tree_tail(dc):
                pi = dc % 3
                bby, bbyB = bbym[pi], B(f"mbby{pi}")
                for lvl in (4, 2):
                    kb.op("dve", lambda e, lvl=lvl, bby=bby: e.tensor_tensor(
                        out=bby[:, 0:lvl, 1:], in0=bby[:, 0:lvl, 1:],
                        in1=bby[:, lvl:2 * lvl, 1:], op=ALU.add),
                        reads=(bbyB,), writes=(bbyB,))
                kb.op("dve", lambda e, dc=dc, bby=bby: e.tensor_tensor(
                    out=ybuf[:, dc, :], in0=bby[:, 0:1, 1:].squeeze(1),
                    in1=bby[:, 1:2, 1:].squeeze(1), op=ALU.add),
                    reads=(bbyB,), writes=(B("mybuf"),))

            bby_in(0)
            pend = None
            for dc in range(DC):
                pi = dc % 3
                ab, abB = abar[pi], B(f"mabar{pi}")
                bby, bbyB = bbym[pi], B(f"mbby{pi}")
                for n in range(NST):
                    kb.op("act", lambda e, dc=dc, n=n, ab=ab: e.activation(
                        out=ab[:, n, 1:], in_=delta[p2][:, dc, :], func=AF.Exp,
                        scale=amat[l][:, dc, n:n + 1]),
                        reads=(dlB, B(f"amat{l}")), writes=(abB,))
                if dc + 1 < DC:
                    bby_in(dc + 1)
                if c == 0:
                    kb.op("dve", lambda e, bby=bby: e.memset(bby[:, :, 0:1], 0.0),
                          writes=(bbyB,), small=True)
                else:
                    kb.op("dve", lambda e, dc=dc, bby=bby: e.tensor_copy(
                        out=bby[:, :, 0:1], in_=carry2[s_][:, dc, :].unsqueeze(2)),
                        reads=(carB,), writes=(bbyB,), small=True)
                kb.op("dve", lambda e, ab=ab, bby=bby: e.tensor_tensor_scan(
                    out=ab[:].rearrange("p n t -> p (n t)"),
                    data0=ab[:].rearrange("p n t -> p (n t)"),
                    data1=bby[:].rearrange("p n t -> p (n t)"),
                    initial=0.0, op0=ALU.mult, op1=ALU.add),
                    reads=(abB, bbyB), writes=(abB,))
                if c < NCH - 1:
                    kb.op("dve", lambda e, dc=dc, ab=ab: e.tensor_copy(
                        out=carry2[s_][:, dc, :], in_=ab[:, :, TCM:TCE].squeeze(2)),
                        reads=(abB,), writes=(carB,), small=True)
                kb.op("dve", lambda e, ab=ab: e.memset(ab[:, :, 0:1], 0.0),
                      writes=(abB,), small=True)
                kb.op("dve", lambda e, ab=ab, bby=bby: e.tensor_tensor(
                    out=bby[:, :, 1:], in0=ab[:, :, 1:], in1=Cb[p2][:],
                    op=ALU.mult),
                    reads=(abB, B(f"mCb{p2}")), writes=(bbyB,))
                if pend is not None:
                    tree_tail(pend)
                kb.op("pool", lambda e, bby=bby: e.tensor_tensor(
                    out=bby[:, 0:8, 1:], in0=bby[:, 0:8, 1:],
                    in1=bby[:, 8:16, 1:], op=ALU.add),
                    reads=(bbyB,), writes=(bbyB,))
                pend = dc
            tree_tail(pend)

        def st_xz(u):
            l, s_, c = U(u)
            p2, p3 = u % 2, u % 3
            xbhp, xbhB = xbh[s_], B(f"mxbh{s_}")
            for oc in range(16):
                stream_w(winb[l], oc, win_sb[oc % 2], B(f"mwin{oc % 2}"))
                pb, pbB = pSv[2 + oc % 2], pSB[2 + oc % 2]
                kb.mm_group([(pb[:, 0:TCM], win_sb[oc % 2][:, kc, :],
                              xn[p2][:, kc, :]) for kc in range(DC)],
                            reads=(B(f"mxn{p2}"), B(f"mwin{oc % 2}")), psb=pbB)
                if oc < 8:
                    kb.op("act", lambda e, oc=oc, pb=pb, xbhp=xbhp: e.activation(
                        out=xbhp[:, oc, KCONV - 1:], in_=pb[:, 0:TCM],
                        func=AF.Copy),
                        reads=(pbB,), writes=(xbhB,))
                else:
                    kb.op("act", lambda e, oc=oc, pb=pb, szp=sz[p3]: e.activation(
                        out=szp[:, oc - 8, :], in_=pb[:, 0:TCM], func=AF.Silu),
                        reads=(pbB,), writes=(B(f"msz{p3}"),))

        def st_wx(u):
            l, s_, c = U(u)
            p2 = u % 2
            s2p, s2B = s2[s_], B(f"ms2{s_}")
            dlB = B(f"mdelta{p2}")
            kb.op("act", lambda e, xcbp=xcb[p2], s2p=s2p: e.activation(
                out=xcbp[:], in_=s2p[:], func=AF.Silu),
                reads=(s2B,), writes=(B(f"mxcb{p2}"),))
            wxps = [(ps2, B("ps2")), (pSv[1], pSB[1])]
            for oc in range(9):
                stream_w(wxb[l], oc, wx_sb[oc % 2], B(f"mwx{oc % 2}"))
                pb, pbB = wxps[oc % 2]
                kb.mm_group([(pb[:, 0:TCM], wx_sb[oc % 2][:, kc, :],
                              xcb[p2][:, kc, :]) for kc in range(DC)],
                            reads=(B(f"mxcb{p2}"), B(f"mwx{oc % 2}")), psb=pbB)
                if oc < 8:
                    kb.op("act", lambda e, oc=oc, pb=pb, s2p=s2p: e.activation(
                        out=s2p[:, oc, :], in_=pb[:, 0:TCM], func=AF.Abs),
                        reads=(pbB,), writes=(s2B,), small=True)
                    kb.op("act", lambda e, oc=oc, pb=pb, dp_=delta[p2]:
                          e.activation(
                        out=dp_[:, oc, :], in_=pb[:, 0:TCM], func=AF.Relu),
                        reads=(pbB,), writes=(dlB,), small=True)
                    kb.op("act", lambda e, oc=oc, sc=cv(-1.0), s2p=s2p:
                          e.activation(
                        out=s2p[:, oc, :], in_=s2p[:, oc, :], func=AF.Exp,
                        scale=sc),
                        reads=(s2B, cvB()), writes=(s2B,), small=True)
                    kb.op("act", lambda e, oc=oc, bi=cv(1.0), s2p=s2p:
                          e.activation(
                        out=s2p[:, oc, :], in_=s2p[:, oc, :], func=AF.Ln,
                        bias=bi),
                        reads=(s2B, cvB()), writes=(s2B,), small=True)
                    kb.op("pool", lambda e, oc=oc, dp_=delta[p2], s2p=s2p:
                          e.tensor_tensor(
                        out=dp_[:, oc, :], in0=dp_[:, oc, :], in1=s2p[:, oc, :],
                        op=ALU.add),
                        reads=(s2B, dlB), writes=(dlB,), small=True)
                else:
                    kb.op("dve", lambda e, pb=pb: e.tensor_copy(
                        out=bcr[p2][:], in_=pb[0:32, 0:TCM]),
                        reads=(pbB,), writes=(B(f"mbcr{p2}"),), small=True)

        def st_dxcb(u):
            p2 = u % 2
            kb.op("dve", lambda e, dxp=dxcb[p2]: e.tensor_tensor(
                out=dxp[:], in0=delta[p2][:], in1=xcb[p2][:], op=ALU.mult),
                reads=(B(f"mdelta{p2}"), B(f"mxcb{p2}")),
                writes=(B(f"mdxcb{p2}"),))

        def st_gate(u):
            l, s_, c = U(u)
            p2, p3 = u % 2, u % 3
            gbp, gbB = gb[p2], B(f"mgb{p2}")
            kb.op("dve", lambda e, gbp=gbp: e.tensor_tensor(
                out=gbp[:], in0=xcb[p2][:],
                in1=dp[l][:].unsqueeze(2).broadcast_to((128, DC, TCM)),
                op=ALU.mult),
                reads=(B(f"mxcb{p2}"), B(f"c_dp{l}")), writes=(gbB,))
            kb.op("dve", lambda e, gbp=gbp: e.tensor_add(
                out=gbp[:], in0=gbp[:], in1=ybuf[:]),
                reads=(B("mybuf"), gbB), writes=(gbB,))
            kb.op("dve", lambda e, gbp=gbp, szp=sz[p3]: e.tensor_mul(
                out=gbp[:], in0=gbp[:], in1=szp[:]),
                reads=(B(f"msz{p3}"), gbB), writes=(gbB,))

        def st_wout(u):
            l, s_, c = U(u)
            p2, p3 = u % 2, u % 3
            xchp, xchB = xch[p3], B(f"mxch{p3}")
            wps = [(pSv[0], pSB[0]), (ps2, B("ps2"))]
            for oc in range(DC):
                stream_w(woutb[l], oc, wo_sb[oc % 2], B(f"mwo{oc % 2}"))
                pb, pbB = wps[oc % 2]
                kb.mm_group([(pb[:, 0:TCM], wo_sb[oc % 2][:, kc, :],
                              gb[p2][:, kc, :]) for kc in range(DC)],
                            reads=(B(f"mgb{p2}"), B(f"mwo{oc % 2}")), psb=pbB)
                kb.op("dve", lambda e, oc=oc, pb=pb, xchp=xchp: e.tensor_add(
                    out=xchp[:, oc, :], in0=xchp[:, oc, :], in1=pb[:, 0:TCM]),
                    reads=(pbB, xchB), writes=(xchB,))
            dst_t, dstB = dsts[l]
            kb.dma("sp", lambda e, xchp=xchp, dst_t=dst_t, s_=s_, c=c:
                   e.dma_start(out=xs_chunk_ap(dst_t, s_, c * TCM, TCM),
                               in_=xchp[:]),
                   reads=(xchB,), writes=(dstB,))

        for r in range(NU + 2):
            uA, uB, uC = r - 2, r - 1, r
            if 0 <= uA < NU:
                st_bcast(uA)
            if uC < NU:
                st_xch_dma(uC)
            if 0 <= uB < NU:
                st_conv(uB)
            if uC < NU:
                st_rms_actpe(uC)
                st_rms_stt(uC)
            if 0 <= uA < NU:
                st_scan(uA)
            if uC < NU:
                st_xz(uC)
            if 0 <= uB < NU:
                st_wx(uB)
                st_dxcb(uB)
            if 0 <= uA < NU:
                st_gate(uA)
                st_wout(uA)

    def phase7(xs_src, srcB):
        AR.reset()
        w1sb = [AR.alloc(f"w1sb{i}", [128, DC, 128], BF16) for i in range(2)]
        w2sb = [AR.alloc(f"w2sb{i}", [128, DC, 128], BF16) for i in range(2)]
        hsb = [AR.alloc(f"hsb{i}", [128, OC_H, 128], BF16) for i in range(2)]
        xln = AR.alloc("xln", [128, DC, TCF], F32)
        xlnb = AR.alloc("xlnb", [128, DC, TCF], BF16)
        t1 = AR.alloc("t1o", [128, OC_H, TCF], BF16)
        t2 = AR.alloc("t2o", [128, OC_H, TCF], BF16)
        vsum = AR.alloc("vsum", [128, DC, TCF], F32)
        vb = AR.alloc("vbo", [128, DC, TCF], BF16)
        sqo = AR.alloc("sqo", [128, DC, TCF], BF16)
        sgcol = AR.alloc("sgcol", [128, TCF], F32)
        yout = AR.alloc("yout", [128, DC, TCF], F32)

        psl = [psS[:, b * 512:(b + 1) * 512] for b in range(4)]
        pslB = [B(f"psS{b}7") for b in range(4)]
        for b in pslB:
            b.psum = True

        def precast(src_dr, dst_dr, noc, nkc):
            srcv = src_dr.ap().rearrange("(kc p) (oc o) -> oc p kc o", p=128, o=128)
            nm = dst_dr.name
            for oc in range(noc):
                if nkc <= 16:
                    kb.dma("pool", lambda e, oc=oc: e.dma_start(
                        out=dst_dr.ap()[oc].rearrange("p (kc o) -> p kc o", o=128),
                        in_=srcv[oc]), writes=(B(nm),), stream="pre7")
                else:
                    for hh in range(2):
                        k0, k1 = hh * nkc // 2, (hh + 1) * nkc // 2
                        kb.dma("pool", lambda e, oc=oc, k0=k0, k1=k1: e.dma_start(
                            out=dst_dr.ap()[oc].rearrange(
                                "p (kc o) -> p kc o", o=128)[:, k0:k1, :],
                            in_=srcv[oc][:, k0:k1, :]), writes=(B(nm),),
                            stream="pre7")
        def ln_rows(v_h, vB, eps):
            kb.op("pool", lambda e: e.tensor_copy(out=vb[:], in_=v_h[:]),
                  reads=(vB,), writes=(B("vbo"),))
            kb.op("act", lambda e: e.activation(out=sqo[:], in_=v_h[:],
                                                func=AF.Square),
                  reads=(vB,), writes=(B("sqo"),))
            kb.mm_group([(psl[0][0:1, 0:TCF], ones_c[:], vb[:, kc, :])
                         for kc in range(DC)],
                        reads=(B("vbo"), B("ones")), psb=pslB[0])
            kb.mm_group([(psl[1][0:1, 0:TCF], ones_c[:], sqo[:, kc, :])
                         for kc in range(DC)],
                        reads=(B("sqo"), B("ones")), psb=pslB[1])
            r0 = rowst[0:1, 0:TCF]
            r1 = rowst[0:1, TCF:2 * TCF]
            r2 = rowst[0:1, 2 * TCF:3 * TCF]
            kb.op("dve", lambda e: e.tensor_scalar_mul(out=r0, in0=psl[0][0:1, 0:TCF],
                                                       scalar1=1.0 / D),
                  reads=(pslB[0],), writes=(B("rowst"),), small=True)
            kb.op("dve", lambda e: e.tensor_scalar_mul(out=r1, in0=psl[1][0:1, 0:TCF],
                                                       scalar1=1.0 / D),
                  reads=(pslB[1],), writes=(B("rowst"),), small=True)
            kb.op("dve", lambda e: e.tensor_tensor(out=r2, in0=r0, in1=r0,
                                                   op=ALU.mult),
                  reads=(B("rowst"),), writes=(B("rowst"),), small=True)
            kb.op("dve", lambda e: e.tensor_sub(out=r1, in0=r1, in1=r2),
                  reads=(B("rowst"),), writes=(B("rowst"),), small=True)
            kb.op("act", lambda e, bi=cv(eps)[0:1, :]: e.activation(
                out=r2, in_=r1, func=AF.Ln, bias=bi),
                  reads=(B("rowst"), cvB()), writes=(B("rowst"),), small=True)
            kb.op("act", lambda e, sc=cv(-0.5)[0:1, :]: e.activation(
                out=r1, in_=r2, func=AF.Exp, scale=sc),
                  reads=(B("rowst"), cvB()), writes=(B("rowst"),), small=True)
            kb.op("dve", lambda e: e.tensor_tensor(out=r2, in0=r0, in1=r1,
                                                   op=ALU.mult),
                  reads=(B("rowst"),), writes=(B("rowst"),), small=True)
            kb.op("dve", lambda e: e.tensor_scalar_mul(out=r2, in0=r2, scalar1=-1.0),
                  reads=(B("rowst"),), writes=(B("rowst"),), small=True)
            kb.mm_group([(psl[2][:, 0:TCF], ones_r32[:], r1)],
                        reads=(B("rowst"), B("ones")), psb=pslB[2])
            kb.mm_group([(psl[3][:, 0:TCF], ones_r32[:], r2)],
                        reads=(B("rowst"), B("ones")), psb=pslB[3])

        def ln_apply(v_h, vB, out_h, outB):
            for dc in range(DC):
                kb.op("dve", lambda e, dc=dc: e.tensor_tensor(
                    out=out_h[:, dc, :], in0=v_h[:, dc, :], in1=psl[2][:, 0:TCF],
                    op=ALU.mult), reads=(vB, pslB[2]), writes=(outB,))
            for dc in range(DC):
                kb.op("dve", lambda e, dc=dc: e.tensor_add(
                    out=out_h[:, dc, :], in0=out_h[:, dc, :], in1=psl[3][:, 0:TCF]),
                    reads=(pslB[3],), writes=(outB,))

        NCH = (NSEQ * T) // TCF
        chl = range(NCH) if _P7CH is None else list(_P7CH)
        for ch in chl:
            s_ = ch // (T // TCF)
            t0 = (ch % (T // TCF)) * TCF
            kb.dma("sp", lambda e, s_=s_, t0=t0: e.dma_start(
                out=vsum[:], in_=xin_chunk_ap(s_, t0, TCF)), writes=(B("vsum"),))
            kb.dma("sp", lambda e, s_=s_, t0=t0: e.dma_start(
                out=yout[:], in_=xs_chunk_ap(xs_src, s_, t0, TCF)),
                reads=(srcB,), writes=(B("yout"),))
            kb.op("dve", lambda e: e.tensor_add(out=vsum[:], in0=vsum[:],
                                                in1=yout[:]),
                  reads=(B("yout"),), writes=(B("vsum"),))
            ln_rows(vsum, B("vsum"), 1e-5)
            ln_apply(vsum, B("vsum"), xln, B("xln"))
            kb.op("pool", lambda e: e.tensor_copy(out=xlnb[:], in_=xln[:]),
                  reads=(B("xln"),), writes=(B("xlnb"),))
            for oc in range(OC_H):
                kb.dma("sp", lambda e, oc=oc, w=w1sb[oc % 2]: e.dma_start(
                    out=w[:], in_=w1b.ap()[oc].rearrange("p (kc o) -> p kc o", o=128)),
                    reads=(B("w1b"),), writes=(B(f"w1sb{oc % 2}"),))
                kb.dma("sp", lambda e, oc=oc, w=w2sb[oc % 2]: e.dma_start(
                    out=w[:], in_=w2b.ap()[oc].rearrange("p (kc o) -> p kc o", o=128)),
                    reads=(B("w2b"),), writes=(B(f"w2sb{oc % 2}"),))
                pg = PSB[(2 * oc) % 3]
                pgB = B(f"ps{(2 * oc) % 3}")
                pu = PSB[(2 * oc + 1) % 3]
                puB = B(f"ps{(2 * oc + 1) % 3}")
                kb.mm_group([(pg[:, 0:TCF], w1sb[oc % 2][:, kc, :], xlnb[:, kc, :])
                             for kc in range(DC)],
                            reads=(B("xlnb"), B(f"w1sb{oc % 2}")), psb=pgB)
                kb.op("act", lambda e, oc=oc, pg=pg: e.activation(
                    out=sgcol[:], in_=pg[:, 0:TCF], func=AF.Silu,
                    scale=s1v[:, oc:oc + 1]),
                    reads=(pgB, B("c_s1")), writes=(B("sgcol"),))
                kb.mm_group([(pu[:, 0:TCF], w2sb[oc % 2][:, kc, :], xlnb[:, kc, :])
                             for kc in range(DC)],
                            reads=(B("xlnb"), B(f"w2sb{oc % 2}")), psb=puB)
                kb.op("dve", lambda e, oc=oc, pu=pu: e.scalar_tensor_tensor(
                    out=t1[:, oc, :], in0=pu[:, 0:TCF], scalar=s2v[:, oc:oc + 1],
                    in1=sgcol[:], op0=ALU.mult, op1=ALU.mult),
                    reads=(puB, B("c_s2"), B("sgcol")), writes=(B("t1o"),))
            for oc in range(OC_H):
                kb.dma("sp", lambda e, oc=oc, w=hsb[oc % 2]: e.dma_start(
                    out=w[:], in_=h1b.ap()[oc].rearrange("p (kc o) -> p kc o", o=128)),
                    reads=(B("h1b"),), writes=(B(f"hsb{oc % 2}"),))
                pb = PSB[oc % 3]
                pbB = B(f"ps{oc % 3}")
                kb.mm_group([(pb[:, 0:TCF], hsb[oc % 2][:, kc, :], t1[:, kc, :])
                             for kc in range(OC_H)],
                            reads=(B("t1o"), B(f"hsb{oc % 2}")), psb=pbB)
                kb.op("act", lambda e, oc=oc, pb=pb: e.activation(
                    out=t2[:, oc, :], in_=pb[:, 0:TCF], func=AF.Gelu,
                    scale=sh1v[:, oc:oc + 1]),
                    reads=(pbB, B("c_sh1")), writes=(B("t2o"),))
            for oc in range(OC_H):
                kb.dma("sp", lambda e, oc=oc, w=hsb[oc % 2]: e.dma_start(
                    out=w[:], in_=h2b.ap()[oc].rearrange("p (kc o) -> p kc o", o=128)),
                    reads=(B("h2b"),), writes=(B(f"hsb{oc % 2}"),))
                pb = PSB[oc % 3]
                pbB = B(f"ps{oc % 3}")
                kb.mm_group([(pb[:, 0:TCF], hsb[oc % 2][:, kc, :], t2[:, kc, :])
                             for kc in range(OC_H)],
                            reads=(B("t2o"), B(f"hsb{oc % 2}")), psb=pbB)
                kb.op("act", lambda e, oc=oc, pb=pb: e.activation(
                    out=t1[:, oc, :], in_=pb[:, 0:TCF], func=AF.Gelu,
                    scale=sh2v[:, oc:oc + 1]),
                    reads=(pbB, B("c_sh2")), writes=(B("t1o"),))
            for oc in range(DC):
                kb.dma("sp", lambda e, oc=oc, w=hsb[oc % 2]: e.dma_start(
                    out=w[:], in_=w3b.ap()[oc].rearrange("p (kc o) -> p kc o", o=128)),
                    reads=(B("w3b"),), writes=(B(f"hsb{oc % 2}"),))
                pb = PSB[oc % 3]
                pbB = B(f"ps{oc % 3}")
                kb.mm_group([(pb[:, 0:TCF], hsb[oc % 2][:, kc, :], t1[:, kc, :])
                             for kc in range(OC_H)],
                            reads=(B("t1o"), B(f"hsb{oc % 2}")), psb=pbB)
                kb.op("dve", lambda e, oc=oc, pb=pb: e.scalar_tensor_tensor(
                    out=vsum[:, oc, :], in0=pb[:, 0:TCF], scalar=s3v[:, oc:oc + 1],
                    in1=xln[:, oc, :], op0=ALU.mult, op1=ALU.add),
                    reads=(pbB, B("c_s3"), B("xln")), writes=(B("vsum"),))
            if _DUMP and ch == 2:
                for nm, h, shp, dt_ in [
                    ("uA", vsum, [128, DC, TCF], F32),
                    ("xlnA", xln, [128, DC, TCF], F32),
                    ("t3A", t1, [128, OC_H, TCF], BF16),
                    ("t2A", t2, [128, OC_H, TCF], BF16),
                ]:
                    dtv = nc.dram_tensor("dbg7_" + nm, shp, dt_, kind="ExternalOutput")
                    kb.dma("sp", lambda e, d=dtv, src=h: e.dma_start(
                        out=d.ap(), in_=src[:]), reads=(B("vsum"), B("xln"),
                        B("t1o"), B("t2o")), out_is_output=True)
            ln_rows(vsum, B("vsum"), 1e-5)
            ln_apply(vsum, B("vsum"), yout, B("yout"))
            import os as _os
            if _os.environ.get("DELAYFIX", "0") == "1":
                kb.op("dve", lambda e: e.tensor_copy(out=sgcol[:, 0:8],
                                                     in_=yout[:, 0, 0:8]),
                      reads=(B("yout"),), writes=(B("sgcol"),), small=True)
                kb.op("dve", lambda e: e.tensor_copy(out=sgcol[:, 8:16],
                                                     in_=sgcol[:, 0:8]),
                      reads=(B("sgcol"),), writes=(B("sgcol"),), small=True)
                kb.dma("sp", lambda e, s_=s_, t0=t0: e.dma_start(
                    out=xs_chunk_ap(y_t, s_, t0, TCF), in_=yout[:]),
                    reads=(B("yout"), B("sgcol")), out_is_output=True)
            else:
                kb.dma("sp", lambda e, s_=s_, t0=t0: e.dma_start(
                    out=xs_chunk_ap(y_t, s_, t0, TCF), in_=yout[:]),
                    reads=(B("yout"),), out_is_output=True)
        if _DUMP:
            for nm, h, shp, dt_ in [
                ("xln", xln, [128, DC, TCF], F32), ("t1", t1, [128, OC_H, TCF], BF16),
                ("t2", t2, [128, OC_H, TCF], BF16), ("vsum", vsum, [128, DC, TCF], F32),
            ]:
                dt2 = nc.dram_tensor("dbg7_" + nm, shp, dt_, kind="ExternalOutput")
                kb.dma("sp", lambda e, d=dt2, src=h: e.dma_start(out=d.ap(), in_=src[:]),
                       reads=(), out_is_output=True)

    def phase2():
        AR.reset()
        wg_h = AR.alloc("wgb", [128, DC, DFFP], BF16)
        wd_h = AR.alloc("wdb", [128, FC, D], BF16)
        wu_sb = [AR.alloc(f"wusb{i}", [128, DC, 128], BF16) for i in range(2)]
        sg = AR.alloc("sg", [128, FC, TCF], F32)
        tb = AR.alloc("tb", [128, FC, TCF], BF16)
        xch = AR.alloc("xch2", [128, DC, TCF], F32)
        xn = AR.alloc("xn2", [128, DC, TCF], BF16)
        sq = AR.alloc("sq2", [128, DC, TCF], BF16)

        wu_src = dr["wu"].ap().rearrange("(kc p) (oc o) -> oc p kc o", p=128, o=128)
        for oc in range(FC):
            kb.dma("pool", lambda e, oc=oc: e.dma_start(
                out=wub.ap()[oc].rearrange("p (kc o) -> p kc o", o=128),
                in_=wu_src[oc]), writes=(B("wub"),), stream="pre2")
        kb.dma("pool", lambda e: e.dma_start(
            out=wg_h[:], in_=dr["wg"].ap().rearrange("(kc p) o -> p kc o", p=128)),
            writes=(B("wgb"),), stream="pre2")
        kb.dma("pool", lambda e: e.dma_start(
            out=wd_h[:], in_=dr["wd"].ap().rearrange("(kc p) o -> p kc o", p=128)),
            writes=(B("wdb"),), stream="pre2")

        for s_ in range(NSEQ):
            for c in range(T // TCF):
                t0 = c * TCF
                kb.dma("sp", lambda e, s_=s_, t0=t0: e.dma_start(
                    out=xch[:], in_=xs_chunk_ap(xs_a, s_, t0, TCF)),
                    reads=(B("xs_a"),), writes=(B("xch2"),))
                rms_norm_chunk(xch, B("xch2"), gffn, B("c_gffn"), xn, B("xn2"), sq, TCF)
                for oc in range(FC):
                    kb.dma("sp", lambda e, oc=oc, w=wu_sb[oc % 2]: e.dma_start(
                        out=w[:], in_=wub.ap()[oc].rearrange("p (kc o) -> p kc o", o=128)),
                        reads=(B("wub"),), writes=(B(f"wusb{oc % 2}"),))
                    pg = PSB[(2 * oc) % 3]
                    pgB = B(f"ps{(2 * oc) % 3}")
                    pu = PSB[(2 * oc + 1) % 3]
                    puB = B(f"ps{(2 * oc + 1) % 3}")
                    kb.mm_group([(pg[:, 0:TCF], wg_h[:, kc, oc * 128:(oc + 1) * 128],
                                  xn[:, kc, :]) for kc in range(DC)],
                                reads=(B("xn2"), B("wgb")), psb=pgB)
                    kb.op("act", lambda e, oc=oc, pg=pg: e.activation(
                        out=sg[:, oc, :], in_=pg[:, 0:TCF], func=AF.Silu),
                        reads=(pgB,), writes=(B("sg"),))
                    kb.mm_group([(pu[:, 0:TCF], wu_sb[oc % 2][:, kc, :],
                                  xn[:, kc, :]) for kc in range(DC)],
                                reads=(B("xn2"), B(f"wusb{oc % 2}")), psb=puB)
                    kb.op("dve", lambda e, oc=oc, pu=pu: e.tensor_tensor(
                        out=tb[:, oc, :], in0=pu[:, 0:TCF], in1=sg[:, oc, :],
                        op=ALU.mult),
                        reads=(puB, B("sg")), writes=(B("tb"),))
                for oc2 in range(DC):
                    pd = PSB[oc2 % 3]
                    pdB = B(f"ps{oc2 % 3}")
                    kb.mm_group([(pd[:, 0:TCF], wd_h[:, kc, oc2 * 128:(oc2 + 1) * 128],
                                  tb[:, kc, :]) for kc in range(FC)],
                                reads=(B("tb"), B("wdb")), psb=pdB)
                    kb.op("dve", lambda e, oc2=oc2, pd=pd: e.tensor_add(
                        out=xch[:, oc2, :], in0=xch[:, oc2, :], in1=pd[:, 0:TCF]),
                        reads=(pdB,), writes=(B("xch2"),))
                kb.dma("sp", lambda e, s_=s_, t0=t0: e.dma_start(
                    out=xs_chunk_ap(xs_b, s_, t0, TCF), in_=xch[:]),
                    reads=(B("xch2"),), writes=(B("xs_b"),))


    def dump(name, h, shape, dt):
        if not _DUMP:
            return
        dt_ = nc.dram_tensor("dbg_" + name, shape, dt, kind="ExternalOutput")
        kb.dma("sp", lambda e, d=dt_, src=h: e.dma_start(out=d.ap(), in_=src[:]),
               reads=(), out_is_output=True)

    if not _ONLY_M0 and not _ONLY_P7:
        phase1()
        kb.barrier()
    xs_cur = "xs_a"
    if _PHASES >= 2 and not _ONLY_M0 and not _ONLY_P7:
        phase2()
        kb.barrier()
        xs_cur = "xs_b"
    xs_map = {"xs_a": xs_a, "xs_b": xs_b}
    if _ONLY_M0:
        phase_mamba_pipe(1, xin_t, B("xin"))
        kb.barrier()
        for s_ in range(NSEQ):
            kb.dma("sp", lambda e, s_=s_: e.dma_start(
                out=y_t.ap()[s_], in_=xs_a.ap()[s_]),
                reads=(B("xs_a"),), out_is_output=True)
        kb.finalize()
        return _replay(nc, kb, ctx)
    nl = min(_PHASES - 2, 3) if _PHASES >= 3 else 0
    if nl > 0 and not _ONLY_P7:
        phase_mamba_pipe(nl, xs_map[xs_cur], B(xs_cur))
        kb.barrier()
        xs_cur = ["xs_a", "xs_b", "xs_a"][nl - 1]

    if _ONLY_P7:
        phase7(xin_t, B("xin"))
        kb.finalize()
        return _replay(nc, kb, ctx)
    if _PHASES >= 7:
        phase7(xs_map[xs_cur], B(xs_cur))

    if _PHASES < 7:
        xs_t = {"xs_a": xs_a, "xs_b": xs_b}[xs_cur]
        for s in range(NSEQ):
            kb.dma("sp", lambda e, s=s, xs_t=xs_t: e.dma_start(
                out=y_t.ap()[s], in_=xs_t.ap()[s]),
                reads=(B(xs_cur),), out_is_output=True)

    kb.finalize()
    return _replay(nc, kb, ctx)


def _replay(nc, kb, ctx):
    sems = {}
    for s in kb.sem_names:
        sems[s] = ctx.enter_context(nc.semaphore("sem_" + s))

    with nc.Block() as block:
        def runner(eng_name):
            def run(engobj):
                seen = {}
                for op in kb.ops[eng_name]:
                    for (s, v) in op.waits:
                        if seen.get(s, -1) >= v:
                            continue
                        seen[s] = v
                        engobj.wait_ge(sems[s], v)
                    if op.fn is None:
                        continue
                    inst = op.fn(engobj)
                    if op.inc is not None:
                        inst.then_inc(sems[op.inc[0]], op.inc[1])
            return run
        block.tensor(runner("pe"))
        block.vector(runner("dve"))
        block.scalar(runner("act"))
        block.gpsimd(runner("pool"))
        block.sync(runner("sp"))

    ctx.close()
    return nc


def kernel(**inp):
    W = _prep_inputs(inp)
    x = np.asarray(inp["L0_attn_x"])
    nc = build_program()
    in_maps = []
    for c in range(8):
        m = dict(W)
        m["xin"] = np.ascontiguousarray(x[2 * c:2 * c + 2].transpose(0, 2, 1))
        in_maps.append(m)
    res = run_bass_kernel_spmd(nc, in_maps, core_ids=list(range(8)))
    outs = [r["y"] for r in res.results]
    y = np.concatenate([o.transpose(0, 2, 1) for o in outs], axis=0)
    return np.ascontiguousarray(y.astype(np.float32))
